# version 06
# baseline (speedup 1.0000x reference)
import contextlib
import os

import numpy as np

import concourse.bass as bass
import concourse.mybir as mybir
from concourse.ap import AP
from concourse.bass_utils import run_bass_kernel_spmd

F32 = mybir.dt.float32
BF16 = mybir.dt.bfloat16
U32 = mybir.dt.uint32
AF = mybir.ActivationFunctionType
ALU = mybir.AluOpType
AX = mybir.AxisListType

T = 4096
D = 1024
NH = 16
HS = 64
NPAIR = NH // 2
FC = D // 128
NR = 2047
TMAX = T - 1
C_SHIFT = np.float32(10.0)
SCALE = np.float32(HS ** -0.5)
MW = 2688
EROW0 = 2048
EPS = np.float32(1e-5)
ITILE = 512
NIT = T // ITILE
NB = T // 128
FTILE = 1024
NFT = T // FTILE
GSIZE = 4
NGRP = NH // GSIZE


def _band_j0s(i0):
    return [j0 for j0 in range(0, T, 128) if j0 <= i0 - 1539]


def _mranges():
    need = {}
    for i0 in range(0, T, ITILE):
        for j0 in _band_j0s(i0):
            for s in range(ITILE // 128):
                i0s = i0 + 128 * s
                lo = max(0, j0 - (i0s + 127) + TMAX)
                hi = j0 + 127 - i0s + TMAX
                if i0s in need:
                    need[i0s][0] = min(need[i0s][0], lo)
                    need[i0s][1] = max(need[i0s][1], hi + 1)
                else:
                    need[i0s] = [lo, hi + 1]
    return {k: tuple(v) for k, v in need.items()}


MRANGES = _mranges()


def build_program(nphase=6):
    SUBMODE = os.environ.get('KSUB', '')
    nc = bass.Bass("TRN2", target_bir_lowering=False)

    def din(name, shape, dt=F32):
        return nc.dram_tensor(name, list(shape), dt, kind="ExternalInput").ap()

    def dint(name, shape, dt):
        return nc.dram_tensor(name, list(shape), dt, kind="Internal").ap()

    xT = din("xT", [D, T])
    wq = din("wq", [D, D]); wqp = din("wqp", [D, D])
    wk = din("wk", [D, D]); wkp = din("wkp", [D, D])
    wv = din("wv", [D, D]); wo = din("wo", [D, D])
    relm = din("relm", [NR, HS])
    qwb = din("qwb", [D, D])
    Wenc = din("Wenc", [2048, D])
    Ws2c = din("Ws2c", [2048, HS])
    Wr_in = din("Wr", [HS, 8])
    W1 = din("W1", [512, 1024])
    W2 = din("W2", [1024, 512])
    Wc2s = din("Wc2s", [HS, 2048])
    Wdec = din("Wdec", [1024, 2048])
    bo2 = din("bo2", [128, FC])
    benc2 = din("benc2", [128, 16])
    bs2c2 = din("bs2c2", [128, 1])
    br_in = din("br", [1, 8])
    b1c = din("b1c", [128, 64])
    b2c = din("b2c", [64, 8])
    bc2s2 = din("bc2s2", [128, 16])
    bdec2 = din("bdec2", [128, FC])
    lng2 = din("lng2", [128, FC]); lnb2 = din("lnb2", [128, FC])
    cos2 = din("cos2", [128, T]); sinf2 = din("sinf2", [128, T])
    idm = din("idm", [128, 128])
    trilm = din("trilm", [128, 128])
    sel8 = din("sel8", [8, 512])

    wq_b = dint("wq_b", [D, D], BF16); wqp_b = dint("wqp_b", [D, D], BF16)
    wk_b = dint("wk_b", [D, D], BF16); wkp_b = dint("wkp_b", [D, D], BF16)
    wv_b = dint("wv_b", [D, D], BF16); wo_b = dint("wo_b", [D, D], BF16)
    qwb_b = dint("qwb_b", [D, D], BF16)
    Wenc_b = dint("Wenc_b", [2048, D], BF16)
    Ws2c_b = dint("Ws2c_b", [2048, HS], BF16)
    W1_b = dint("W1_b", [512, 1024], BF16)
    W2_b = dint("W2_b", [1024, 512], BF16)
    Wc2s_b = dint("Wc2s_b", [HS, 2048], BF16)
    Wdec_b = dint("Wdec_b", [1024, 2048], BF16)
    cos_b = dint("cos_b", [128, T], BF16)
    sinf_b = dint("sinf_b", [128, T], BF16)
    rel_tmp = dint("rel_tmp", [64, MW], BF16)
    qrc = dint("qrc", [NH * (T - EROW0), MW], BF16)
    ocat = dint("ocat", [D, T], BF16)
    xres = dint("xres", [D, T], F32)
    outT = nc.dram_tensor("outT", [D, T], F32, kind="ExternalOutput").ap()

    es = contextlib.ExitStack()

    def sbuf(name, shape, dt):
        return es.enter_context(nc.sbuf_tensor(name, list(shape), dt))

    XH = [sbuf(f"xh{c}", [128, T], BF16) for c in range(FC)]
    A = [sbuf(f"ar{c}", [128, 4160], BF16) for c in range(FC)]
    RT2 = sbuf("rt2", [128, MW], BF16)
    QR = sbuf("qr", [128, ITILE], BF16)
    PB = [sbuf(f"pb{i}", [128, 2 * ITILE], BF16) for i in range(2)]
    CTA = [sbuf(f"cta{i}", [128, ITILE], BF16) for i in range(2)]
    CTB = [sbuf(f"ctb{i}", [128, ITILE], BF16) for i in range(2)]
    DG = [sbuf(f"dg{i}", [128, 128], BF16) for i in range(4)]
    SSB = [sbuf(f"ssb{i}", [128, ITILE], F32) for i in range(2)]
    WQS = sbuf("wqs", [128, 1024], BF16)
    WQPS = sbuf("wqps", [128, 1024], BF16)
    WVG = sbuf("wvg", [128, GSIZE * HS * FC], BF16)
    WF = [sbuf(f"wf{i}", [128, 1024], BF16) for i in range(2)]
    WG = [sbuf(f"wg{i}", [128, 1024], BF16) for i in range(2)]
    WG2 = [sbuf(f"wg2{i}", [128, 512], BF16) for i in range(2)]
    CONT2 = sbuf("cont2", [128, FTILE], BF16)
    EOACC = sbuf("eoacc", [64, FTILE], F32)
    GREP = sbuf("grep", [64, 512], F32)
    ET0 = sbuf("et0", [64, 512], F32)
    GTT = sbuf("gtt", [8, FTILE], BF16)
    SEL = sbuf("sel", [8, 512], BF16)
    OA = sbuf("oa", [64, ITILE], BF16)
    OB = sbuf("ob", [64, ITILE], BF16)
    LA = sbuf("la", [1, ITILE], F32)
    LB = sbuf("lb", [1, ITILE], F32)
    XOLD = [sbuf(f"xold{i}", [128, 512], F32) for i in range(2)]
    RREP = sbuf("rrep", [128, 512], BF16)
    MRREP = sbuf("mrrep", [128, 512], BF16)
    SQ = [sbuf(f"sq{i}", [128, 512], BF16) for i in range(2)]
    MU = sbuf("mu", [1, 512], F32); VAR = sbuf("var", [1, 512], F32)
    ONES1 = sbuf("ones1", [1, 128], BF16)
    ONESC = sbuf("onesc", [128, 1], BF16)
    IDT = sbuf("idt", [128, 128], BF16)
    TRIL = sbuf("tril", [128, 128], F32)
    B1C = sbuf("b1c_s", [128, 64], F32)
    B2C = sbuf("b2c_s", [64, 8], F32)
    BC2S = sbuf("bc2s_s", [128, 16], F32)
    BDEC = sbuf("bdec_s", [128, FC], F32)
    LNG = sbuf("lng_s", [128, FC], F32)
    LNB = sbuf("lnb_s", [128, FC], F32)
    BO = sbuf("bo_s", [128, FC], F32)
    BENC = sbuf("benc_s", [128, 16], F32)
    BS2C = sbuf("bs2c_s", [128, 1], F32)
    WRX = sbuf("wrx", [64, 8], BF16)
    BRX = sbuf("brx", [1, 8], BF16)
    GT = sbuf("gt", [128, 64], F32)
    PEXP = sbuf("pexp", [128, 8], F32)
    M8V = sbuf("m8v", [128, 8], F32)
    M8I = sbuf("m8i", [128, 8], U32)
    GM = [sbuf(f"gm{i}", [128, 1], F32) for i in range(3)]
    GB8 = sbuf("gb8", [128, 8], BF16)
    QB = sbuf("qb", [128, ITILE], BF16)
    U1 = sbuf("u1", [128, 512], BF16)
    U2 = sbuf("u2", [128, 512], BF16)
    CST = [sbuf("cst0", [128, 512], BF16)]
    SNT = [sbuf("snt0", [128, 512], BF16)]
    ROW1 = sbuf("row1", [1, 512], BF16)
    ROW2 = sbuf("row2", [1, 512], BF16)

    PS01 = es.enter_context(nc.psum_tensor("ps01", [128, 1024], F32))
    PS23 = es.enter_context(nc.psum_tensor("ps23", [128, 1024], F32))
    PSO_A = es.enter_context(nc.psum_tensor("psoa", [128, 512], F32))
    PSO_B = es.enter_context(nc.psum_tensor("psob", [128, 512], F32))
    PSX = es.enter_context(nc.psum_tensor("psx", [128, 512], F32))
    PSY = es.enter_context(nc.psum_tensor("psy", [128, 512], F32))

    PSB = PSO_B[:, 0:64].bitcast(BF16)

    class Sem:
        __slots__ = ("h", "n")

    def S(name):
        s = Sem(); s.h = es.enter_context(nc.semaphore(name)); s.n = 0
        return s

    s_ld = S("s_ld"); s_gp = S("s_gp"); s_st = S("s_st")
    s_dg = S("s_dg"); s_tp = S("s_tp")
    s_pe = S("s_pe"); s_ac = S("s_ac"); s_ve = S("s_ve")

    es.enter_context(nc.Block())

    def inc(sem, instr, k=1):
        instr.then_inc(sem.h, k)
        sem.n += k
        return sem.n

    def w(eng, sem, val=None):
        v = sem.n if val is None else val
        if v:
            eng.wait_ge(sem.h, v)

    def dma(eng, out, in_, sem, **kw):
        eng.dma_start(out=out, in_=in_, **kw).then_inc(sem.h, 16)
        sem.n += 16
        return sem.n

    def fence():
        for eng in (nc.tensor, nc.vector, nc.scalar, nc.sync, nc.gpsimd):
            for sem in (s_ld, s_gp, s_st, s_dg, s_tp, s_pe, s_ac, s_ve):
                if sem.n:
                    eng.wait_ge(sem.h, sem.n)

    for dst, src in [(wq_b, wq), (wqp_b, wqp), (wk_b, wk), (wkp_b, wkp),
                     (wv_b, wv), (wo_b, wo), (qwb_b, qwb), (Wenc_b, Wenc),
                     (Ws2c_b, Ws2c), (W1_b, W1), (W2_b, W2), (Wc2s_b, Wc2s),
                     (Wdec_b, Wdec), (cos_b, cos2), (sinf_b, sinf2)]:
        dma(nc.gpsimd, dst[:], src[:], s_gp)
    dma(nc.gpsimd, IDT[:], idm[:], s_gp)
    dma(nc.gpsimd, WRX[:], Wr_in[:], s_gp)
    dma(nc.gpsimd, BRX[:], br_in[:], s_gp)
    dma(nc.gpsimd, SEL[:], sel8[:], s_gp)
    for dst, src in [(TRIL, trilm), (BO, bo2), (BENC, benc2), (BS2C, bs2c2),
                     (B1C, b1c), (B2C, b2c), (BC2S, bc2s2), (BDEC, bdec2),
                     (LNG, lng2), (LNB, lnb2)]:
        dma(nc.sync, dst[:], src[:], s_ld)
    w(nc.vector, s_ld)
    nc.vector.memset(ONES1[:], 1.0)
    inc(s_ve, nc.vector.memset(ONESC[:], 1.0))

    w(nc.gpsimd, s_gp)
    w(nc.vector, s_ld)
    inc(s_ve, nc.vector.memset(A[0][:, 0:1024], 0.0))
    w(nc.gpsimd, s_ve)
    for t in range(16):
        r0 = 128 * t
        rcnt = min(128, NR - r0)
        dma(nc.gpsimd, A[0][0:rcnt, 64 * t:64 * t + 64], relm[r0:r0 + rcnt, :], s_gp)
    w(nc.vector, s_gp)
    inc(s_ve, nc.vector.memset(RT2[:], 0.0))
    for t in range(16):
        w(nc.tensor, s_gp)
        w(nc.tensor, s_ve)
        inc(s_pe, nc.tensor.transpose(PSB[0:64, :], A[0][:, 64 * t:64 * t + 64], IDT[:]))
        w(nc.vector, s_pe)
        inc(s_ve, nc.vector.tensor_copy(RT2[0:64, 128 * t:128 * t + 128], PSB[0:64, :]))
        w(nc.tensor, s_ve)
    w(nc.vector, s_ve)
    inc(s_ve, nc.vector.tensor_copy(GM[0][0:64, :], RT2[0:64, 2046:2047]))
    w(nc.vector, s_ve)
    inc(s_ve, nc.vector.tensor_scalar(out=RT2[0:64, 0:NR], in0=RT2[0:64, 0:NR],
                                      scalar1=GM[0][0:64, :], scalar2=float(1.0 / float(SCALE)),
                                      op0=ALU.subtract, op1=ALU.mult))
    w(nc.sync, s_ve)
    dma(nc.sync, rel_tmp[:], RT2[0:64, :], s_st)
    w(nc.sync, s_st)
    dma(nc.sync, RT2[64:128, :], rel_tmp[:], s_ld)
    fence()

    def stats_tile(read_tile, write_t):
        for c in range(FC):
            w(nc.tensor, s_ve)
            mm = nc.tensor.matmul(PSX[0:1, :], ONESC[:], read_tile(c),
                                  start=(c == 0), stop=(c == FC - 1), skip_group_check=True)
        inc(s_pe, mm)
        w(nc.vector, s_pe)
        inc(s_ve, nc.vector.tensor_scalar_mul(MU[0:1, :], PSX[0:1, :], 1.0 / D))
        w(nc.vector, s_ve)
        inc(s_ve, nc.vector.tensor_copy(ROW2[0:1, :], MU[0:1, :]))
        w(nc.tensor, s_ve)
        mm = nc.tensor.matmul(PSY[:, :], ONES1[:], ROW2[0:1, :], start=True, stop=True,
                              skip_group_check=True)
        inc(s_pe, mm)
        w(nc.vector, s_pe)
        inc(s_ve, nc.vector.tensor_copy(MRREP[:, :], PSY[:, :]))
        w(nc.vector, s_ve)
        for c in range(FC):
            t1 = nc.vector.tensor_tensor(out=write_t(c), in0=read_tile(c), in1=MRREP[:, :],
                                         op=ALU.subtract)
        sub_tok = inc(s_ve, t1)
        for c in range(FC):
            w(nc.scalar, s_ve, sub_tok)
            w(nc.scalar, s_pe)
            inc(s_ac, nc.scalar.activation(out=SQ[c % 2][:], in_=write_t(c), func=AF.Square))
            w(nc.tensor, s_ac)
            mm = nc.tensor.matmul(PSY[0:1, :], ONESC[:], SQ[c % 2][:],
                                  start=(c == 0), stop=(c == FC - 1), skip_group_check=True)
            inc(s_pe, mm)
        w(nc.vector, s_pe)
        inc(s_ve, nc.vector.tensor_scalar_mul(VAR[0:1, :], PSY[0:1, :], 1.0 / D))
        w(nc.scalar, s_ve)
        inc(s_ac, nc.scalar.activation(out=VAR[0:1, :], in_=VAR[0:1, :], func=AF.Sqrt, bias=EPS))
        w(nc.vector, s_ac)
        inc(s_ve, nc.vector.reciprocal(VAR[0:1, :], VAR[0:1, :]))
        w(nc.vector, s_ve)
        inc(s_ve, nc.vector.tensor_copy(ROW1[0:1, :], VAR[0:1, :]))
        w(nc.tensor, s_ve)
        mm = nc.tensor.matmul(PSX[:, :], ONES1[:], ROW1[0:1, :], start=True, stop=True,
                              skip_group_check=True)
        inc(s_pe, mm)
        w(nc.vector, s_pe)
        inc(s_ve, nc.vector.tensor_copy(RREP[:, :], PSX[:, :]))
        w(nc.tensor, s_ve)

    def ln_pass(src_dram):
        for c in range(FC):
            dma(nc.gpsimd, A[c][:, 0:T], src_dram[128 * c:128 * c + 128, :], s_gp)
        xt_done = s_gp.n
        for eng in (nc.scalar, nc.tensor, nc.vector):
            w(eng, s_gp, xt_done)
        for tt in range(T // 512):
            sl = slice(512 * tt, 512 * tt + 512)
            stats_tile(lambda c: A[c][:, sl], lambda c: XH[c][:, sl])
            w(nc.vector, s_ve)
            for c in range(FC):
                t2 = nc.vector.tensor_tensor(out=XH[c][:, sl], in0=XH[c][:, sl],
                                             in1=RREP[:, :], op=ALU.mult)
            inc(s_ve, t2)
            w(nc.vector, s_ve)
        fence()

    def proj_rot_tile(wmain, wperm, dst_tile, sl, ttpar):
        cb = CST[0]; sn = SNT[0]
        w(nc.sync, s_ve)
        dma(nc.sync, cb[:], cos_b[:, sl], s_ld)
        dma(nc.sync, sn[:], sinf_b[:, sl], s_ld)
        cs_tok = s_ld.n
        w(nc.tensor, s_ld)
        w(nc.tensor, s_ve)
        for c in range(FC):
            nc.tensor.matmul(PSX[:, :], wmain[:, 128 * c:128 * c + 128], XH[c][:, sl],
                             start=(c == 0), stop=(c == FC - 1), skip_group_check=True)
            mm = nc.tensor.matmul(PSY[:, :], wperm[:, 128 * c:128 * c + 128], XH[c][:, sl],
                                  start=(c == 0), stop=(c == FC - 1), skip_group_check=True)
        inc(s_pe, mm)
        w(nc.vector, s_pe)
        w(nc.vector, s_ld, cs_tok)
        w(nc.vector, s_st)
        nc.vector.tensor_tensor(out=U1[:, :], in0=PSX[:, :], in1=cb[:], op=ALU.mult)
        inc(s_ve, nc.vector.tensor_tensor(out=U2[:, :], in0=PSY[:, :], in1=sn[:], op=ALU.mult))
        w(nc.vector, s_ve)
        inc(s_ve, nc.vector.tensor_tensor(out=dst_tile, in0=U1[:, :], in1=U2[:, :], op=ALU.add))

    def proj_pair_rot(wmain, wperm, dst):
        for tt in range(T // 512):
            proj_rot_tile(wmain, wperm, dst[:, 512 * tt:512 * tt + 512],
                          slice(512 * tt, 512 * tt + 512), tt)

    def attn_a():
        for grp in range(NGRP):
            pairs = list(range((GSIZE // 2) * grp, (GSIZE // 2) * grp + GSIZE // 2))
            for pi, pr in enumerate(pairs):
                w(nc.sync, s_pe)
                dma(nc.sync, WQS[:], wk_b[128 * pr:128 * pr + 128, :], s_ld)
                dma(nc.sync, WQPS[:], wkp_b[128 * pr:128 * pr + 128, :], s_ld)
                proj_pair_rot(WQS, WQPS, A[pi])
            w(nc.vector, s_ve)
            for pi in range(GSIZE // 2):
                v3 = A[GSIZE // 2 + pi][:, 0:NB * 130].rearrange("p (b q) -> p b q", q=130)
                nc.vector.memset(v3[:, :, 64:65], 1.0)
                ms = nc.vector.memset(v3[:, :, 129:130], 1.0)
            inc(s_ve, ms)
            ones_tok = s_ve.n
            gc0 = GSIZE * HS * grp
            w(nc.sync, s_pe)
            for c in range(FC):
                dma(nc.sync, WVG[:, 256 * c:256 * c + 256], wv_b[128 * c:128 * c + 128, gc0:gc0 + 256], s_ld)
            wv_tok = s_ld.n
            for tb in range(NB):
                w(nc.tensor, s_ld, wv_tok)
                w(nc.tensor, s_ve)
                for c in range(FC):
                    mm = nc.tensor.matmul(PSX[:, 0:256], XH[c][:, 128 * tb:128 * tb + 128],
                                          WVG[:, 256 * c:256 * c + 256],
                                          start=(c == 0), stop=(c == FC - 1), skip_group_check=True)
                inc(s_pe, mm)
                w(nc.vector, s_pe)
                w(nc.vector, s_ve, ones_tok)
                for pi in range(GSIZE // 2):
                    vA = A[GSIZE // 2 + pi]
                    nc.vector.tensor_copy(vA[:, 130 * tb:130 * tb + 64], PSX[:, 128 * pi:128 * pi + 64])
                    cp = nc.vector.tensor_copy(vA[:, 130 * tb + 65:130 * tb + 129],
                                               PSX[:, 128 * pi + 64:128 * pi + 128])
                inc(s_ve, cp)
            fence()

            for pi, pr in enumerate(pairs):
                kA = A[pi]
                vA = A[GSIZE // 2 + pi]
                hA, hB = 2 * pr, 2 * pr + 1
                w(nc.sync, s_pe)
                dma(nc.sync, WQS[:], wq_b[128 * pr:128 * pr + 128, :], s_ld)
                dma(nc.sync, WQPS[:], wqp_b[128 * pr:128 * pr + 128, :], s_ld)
                for it in range(NIT):
                    i0 = ITILE * it
                    proj_rot_tile(WQS, WQPS, QR[:, :], slice(i0, i0 + ITILE), it)
                    qr_tok = s_ve.n
                    for s4 in range(ITILE // 128):
                        i0s = i0 + 128 * s4
                        if i0s not in MRANGES:
                            continue
                        mlo, mhi = MRANGES[i0s]
                        mt = (mlo // 512) * 512
                        while mt < mhi:
                            mwd = min(512, MW - mt)
                            w(nc.tensor, s_ve, qr_tok)
                            w(nc.tensor, s_ac)
                            nc.tensor.matmul(PSX[:, 0:mwd], QR[0:64, 128 * s4:128 * s4 + 128],
                                             RT2[0:64, mt:mt + mwd],
                                             start=True, stop=True, tile_position=(0, 0), skip_group_check=True)
                            mm = nc.tensor.matmul(PSY[:, 0:mwd], QR[64:128, 128 * s4:128 * s4 + 128],
                                                  RT2[64:128, mt:mt + mwd],
                                                  start=True, stop=True, tile_position=(64, 0), skip_group_check=True)
                            inc(s_pe, mm)
                            w(nc.scalar, s_pe)
                            w(nc.scalar, s_ve, qr_tok)
                            w(nc.scalar, s_st)
                            nc.scalar.copy(U1[:, 0:mwd], PSX[:, 0:mwd])
                            cp = nc.scalar.copy(U2[:, 0:mwd], PSY[:, 0:mwd])
                            inc(s_ac, cp)
                            w(nc.gpsimd, s_ac)
                            r0 = hA * (T - EROW0) + (i0s - EROW0)
                            dma(nc.gpsimd, qrc[r0:r0 + 128, mt:mt + mwd], U1[:, 0:mwd], s_st)
                            r0 = hB * (T - EROW0) + (i0s - EROW0)
                            dma(nc.gpsimd, qrc[r0:r0 + 128, mt:mt + mwd], U2[:, 0:mwd], s_st)
                            mt += mwd
                    qrc_st_tok = s_st.n
                    band_list = _band_j0s(i0)
                    bandj = set(band_list)
                    ct_tok = {}
                    dve_band_tok = {}

                    def prefetch(bi):
                        j0 = band_list[bi]
                        r = bi % 2
                        if bi >= 2:
                            w(nc.sync, s_ve, dve_band_tok[bi - 2])
                        w(nc.sync, s_st, qrc_st_tok)
                        for head, ctb in ((hA, CTA[r]), (hB, CTB[r])):
                            base = head * (T - EROW0)
                            w(nc.sync, s_tp)
                            for s in range(4):
                                i0s = i0 + 128 * s
                                off = (base + i0s - EROW0) * MW + (j0 - i0s + TMAX)
                                dma(nc.sync, DG[s][:], AP(qrc.tensor, off, [[MW - 1, 128], [1, 128]]), s_dg)
                            w(nc.sync, s_dg)
                            for s in range(4):
                                dma(nc.sync, ctb[:, 128 * s:128 * s + 128], DG[s][:], s_tp, transpose=True)
                        ct_tok[j0] = s_tp.n

                    for bi in range(min(2, len(band_list))):
                        prefetch(bi)

                    w(nc.tensor, s_ac)
                    w(nc.tensor, s_ve, qr_tok)
                    w(nc.scalar, s_pe)
                    exp_tok = {}
                    av_tok = {}
                    for jb in range(NB):
                        j0 = 128 * jb
                        sc = PS01 if jb % 2 == 0 else PS23
                        if jb >= 2:
                            w(nc.tensor, s_ac, exp_tok[jb - 2])
                        nc.tensor.matmul(sc[:, 0:512], kA[0:64, j0:j0 + 128], QR[0:64, :],
                                         start=True, stop=True, tile_position=(0, 0), skip_group_check=True)
                        mm = nc.tensor.matmul(sc[:, 512:1024], kA[64:128, j0:j0 + 128], QR[64:128, :],
                                              start=True, stop=True, tile_position=(64, 0), skip_group_check=True)
                        qk_tok = inc(s_pe, mm)
                        is_band = j0 in bandj
                        if is_band:
                            bi = band_list.index(j0)
                            r = bi % 2
                            w(nc.vector, s_pe, qk_tok)
                            w(nc.vector, s_tp, ct_tok[j0])
                            w(nc.vector, s_ac)
                            nc.vector.tensor_tensor(out=SSB[0][:, :], in0=sc[:, 0:512], in1=CTA[r][:], op=ALU.add)
                            bm = nc.vector.tensor_tensor(out=SSB[1][:, :], in0=sc[:, 512:1024], in1=CTB[r][:], op=ALU.add)
                            band_tok = inc(s_ve, bm)
                            dve_band_tok[bi] = band_tok
                            if bi + 2 < len(band_list):
                                prefetch(bi + 2)
                        pb = PB[jb % 2]
                        w(nc.scalar, s_pe, qk_tok)
                        if jb >= 2:
                            w(nc.scalar, s_pe, av_tok[jb - 2])
                        if is_band:
                            w(nc.scalar, s_ve, band_tok)
                            nc.scalar.activation(out=pb[:, 0:512], in_=SSB[0][:, :], func=AF.Exp,
                                                 bias=-C_SHIFT, scale=SCALE)
                            ex = nc.scalar.activation(out=pb[:, 512:1024], in_=SSB[1][:, :], func=AF.Exp,
                                                      bias=-C_SHIFT, scale=SCALE)
                        else:
                            ex = nc.scalar.activation(out=pb[:, :], in_=sc[:, :], func=AF.Exp,
                                                      bias=-C_SHIFT, scale=SCALE)
                        exp_tok[jb] = inc(s_ac, ex)
                        w(nc.tensor, s_ac, exp_tok[jb])
                        nc.tensor.matmul(PSO_A[0:65, :], vA[:, 130 * jb:130 * jb + 65], pb[:, 0:512],
                                         start=(jb == 0), stop=(jb == NB - 1), skip_group_check=True)
                        mm = nc.tensor.matmul(PSO_B[0:65, :], vA[:, 130 * jb + 65:130 * jb + 130], pb[:, 512:1024],
                                              start=(jb == 0), stop=(jb == NB - 1), skip_group_check=True)
                        av_tok[jb] = inc(s_pe, mm)
                    w(nc.vector, s_pe, av_tok[NB - 1])
                    nc.vector.tensor_copy(LA[0:1, :], PSO_A[64:65, :])
                    inc(s_ve, nc.vector.tensor_copy(LB[0:1, :], PSO_B[64:65, :]))
                    w(nc.vector, s_ve)
                    nc.vector.reciprocal(LA[0:1, :], LA[0:1, :])
                    inc(s_ve, nc.vector.reciprocal(LB[0:1, :], LB[0:1, :]))
                    w(nc.vector, s_ve)
                    nc.vector.tensor_copy(ROW1[0:1, :], LA[0:1, :])
                    inc(s_ve, nc.vector.tensor_copy(ROW2[0:1, :], LB[0:1, :]))
                    w(nc.tensor, s_ve)
                    nc.tensor.matmul(PSX[0:64, :], ONES1[0:1, 0:64], ROW1[0:1, :], start=True, stop=True,
                                     skip_group_check=True)
                    mm = nc.tensor.matmul(PSY[0:64, :], ONES1[0:1, 0:64], ROW2[0:1, :], start=True, stop=True,
                                          skip_group_check=True)
                    inc(s_pe, mm)
                    w(nc.vector, s_pe)
                    nc.vector.tensor_copy(SSB[1][0:64, :], PSX[0:64, :])
                    inc(s_ve, nc.vector.tensor_copy(SSB[0][0:64, :], PSY[0:64, :]))
                    w(nc.vector, s_ve)
                    w(nc.vector, s_st)
                    nc.vector.tensor_tensor(out=OA[:, :], in0=PSO_A[0:64, :], in1=SSB[1][0:64, :], op=ALU.mult)
                    ot = nc.vector.tensor_tensor(out=OB[:, :], in0=PSO_B[0:64, :], in1=SSB[0][0:64, :], op=ALU.mult)
                    inc(s_ve, ot)
                    w(nc.gpsimd, s_ve)
                    dma(nc.gpsimd, ocat[HS * hA:HS * hA + HS, i0:i0 + ITILE], OA[:, :], s_st)
                    dma(nc.gpsimd, ocat[HS * hB:HS * hB + HS, i0:i0 + ITILE], OB[:, :], s_st)
                    w(nc.tensor, s_ve)
            fence()

        for tt in range(T // 512):
            sl = slice(512 * tt, 512 * tt + 512)
            for c in range(FC):
                w(nc.sync, s_pe)
                dma(nc.sync, A[4 + (c % 2)][:, 512 * (c // 2):512 * (c // 2) + 512],
                    ocat[128 * c:128 * c + 128, sl], s_ld)
            oc_tok = s_ld.n
            for od in range(FC):
                w(nc.sync, s_pe)
                dma(nc.sync, WG[od % 2][:], wo_b[128 * od:128 * od + 128, :], s_ld)
                wtok = s_ld.n
                w(nc.tensor, s_ld, wtok)
                w(nc.tensor, s_ld, oc_tok)
                pso = PSX if od % 2 == 0 else PSY
                if od >= 2:
                    w(nc.tensor, s_ve)
                for c in range(FC):
                    mm = nc.tensor.matmul(pso[:, :], WG[od % 2][:, 128 * c:128 * c + 128],
                                          A[4 + (c % 2)][:, 512 * (c // 2):512 * (c // 2) + 512],
                                          start=(c == 0), stop=(c == FC - 1), skip_group_check=True)
                inc(s_pe, mm)
                xb = XOLD[od % 2]
                w(nc.sync, s_ve)
                dma(nc.sync, xb[:], xT[128 * od:128 * od + 128, sl], s_ld)
                xtok = s_ld.n
                w(nc.vector, s_pe)
                w(nc.vector, s_ld, xtok)
                ad = nc.vector.scalar_tensor_tensor(out=xb[:], in0=pso[:, :], scalar=BO[:, od:od + 1],
                                                    in1=xb[:], op0=ALU.add, op1=ALU.add)
                inc(s_ve, ad)
                w(nc.gpsimd, s_ve)
                dma(nc.gpsimd, xres[128 * od:128 * od + 128, sl], xb[:], s_st)
        fence()

    def ffn(dst_dram):
        for ft in range(NFT):
            for hc in range(16):
                wfb = WF[hc % 2]
                w(nc.sync, s_pe)
                dma(nc.sync, wfb[:], Wenc_b[128 * hc:128 * hc + 128, :], s_ld)
                wtok = s_ld.n
                w(nc.tensor, s_ld, wtok)
                if hc >= 2:
                    w(nc.tensor, s_ac)
                for hh in range(2):
                    hsl = slice(FTILE * ft + 512 * hh, FTILE * ft + 512 * hh + 512)
                    for c in range(FC):
                        mm = nc.tensor.matmul(PS01[:, 512 * hh:512 * hh + 512],
                                              wfb[:, 128 * c:128 * c + 128], XH[c][:, hsl],
                                              start=(c == 0), stop=(c == FC - 1), skip_group_check=True)
                inc(s_pe, mm)
                w(nc.scalar, s_pe)
                sdst = A[hc // 2][:, 1024 * (hc % 2):1024 * (hc % 2) + 1024]
                sg = nc.scalar.activation(out=sdst, in_=PS01[:, :], func=AF.Sigmoid,
                                          bias=BENC[:, hc:hc + 1])
                inc(s_ac, sg)
            for hc in range(16):
                wgb = WG2[hc % 2]
                if hc >= 2:
                    w(nc.sync, s_pe)
                dma(nc.sync, wgb[:, 0:64], Ws2c_b[128 * hc:128 * hc + 128, 0:64], s_ld)
                wtok = s_ld.n
                w(nc.tensor, s_ld, wtok)
                w(nc.tensor, s_ac)
                if hc == 0:
                    w(nc.tensor, s_ve)
                ssrc = A[hc // 2][:, 1024 * (hc % 2):1024 * (hc % 2) + 1024]
                for hh in range(2):
                    nc.tensor.matmul(PS23[0:64, 512 * hh:512 * hh + 512], wgb[:, 0:64],
                                     ssrc[:, 512 * hh:512 * hh + 512],
                                     start=(hc == 0), stop=(hc == 15),
                                     tile_position=(0, 0), skip_group_check=True)
                    mm = nc.tensor.matmul(PS01[64:128, 512 * hh:512 * hh + 512], wgb[:, 0:64],
                                          ssrc[:, 512 * hh:512 * hh + 512],
                                          start=(hc == 0), stop=(hc == 15),
                                          tile_position=(0, 64), skip_group_check=True)
                inc(s_pe, mm)
            w(nc.vector, s_pe)
            nc.vector.tensor_scalar(out=CONT2[0:64, :], in0=PS23[0:64, :], scalar1=BS2C[0:64, 0:1],
                                    scalar2=None, op0=ALU.add)
            cv = nc.vector.tensor_scalar(out=CONT2[64:128, :], in0=PS01[64:128, :], scalar1=BS2C[64:128, 0:1],
                                         scalar2=None, op0=ALU.add)
            inc(s_ve, cv)
            cont_tok = s_ve.n
            for tb in range(8):
                bsl = slice(128 * tb, 128 * tb + 128)
                w(nc.tensor, s_ve, cont_tok)
                if tb >= 1:
                    w(nc.tensor, s_ac)
                    w(nc.tensor, s_ve)
                nc.tensor.matmul(PSX[:, 0:8], CONT2[0:64, bsl], WRX[:, :], start=True, stop=False,
                                 skip_group_check=True)
                mm = nc.tensor.matmul(PSX[:, 0:8], ONES1[0:1, 0:128], BRX[0:1, :], start=False, stop=True,
                                      skip_group_check=True)
                inc(s_pe, mm)
                w(nc.vector, s_pe)
                inc(s_ve, nc.vector.reduce_max(GM[0][:, :], PSX[:, 0:8], axis=AX.X))
                w(nc.vector, s_ve)
                inc(s_ve, nc.vector.tensor_scalar_mul(GM[1][:, :], GM[0][:, :], -1.0))
                w(nc.scalar, s_ve)
                ex = nc.scalar.activation(out=PEXP[:, :], in_=PSX[:, 0:8], func=AF.Exp,
                                          bias=GM[1][:, 0:1])
                inc(s_ac, ex)
                w(nc.vector, s_ac)
                nc.vector.max_with_indices(M8V[:, :], M8I[:, :], PEXP[:, :])
                inc(s_ve, nc.vector.tensor_copy(GM[2][:, :], M8V[:, 1:2]))
                w(nc.vector, s_ve)
                inc(s_ve, nc.vector.tensor_tensor(out=GM[0][:, :], in0=M8V[:, 0:1], in1=M8V[:, 1:2], op=ALU.add))
                w(nc.vector, s_ve)
                inc(s_ve, nc.vector.reciprocal(GM[0][:, :], GM[0][:, :]))
                w(nc.vector, s_ve)
                inc(s_ve, nc.vector.tensor_scalar(out=GT[:, 8 * tb:8 * tb + 8], in0=PEXP[:, :],
                                                  scalar1=GM[2][:, 0:1], scalar2=None, op0=ALU.is_ge))
                w(nc.vector, s_ve)
                inc(s_ve, nc.vector.tensor_tensor(out=GT[:, 8 * tb:8 * tb + 8], in0=GT[:, 8 * tb:8 * tb + 8],
                                                  in1=PEXP[:, :], op=ALU.mult))
                w(nc.vector, s_ve)
                inc(s_ve, nc.vector.tensor_scalar(out=GT[:, 8 * tb:8 * tb + 8], in0=GT[:, 8 * tb:8 * tb + 8],
                                                  scalar1=GM[0][:, 0:1], scalar2=None, op0=ALU.mult))
                w(nc.vector, s_ve)
                inc(s_ve, nc.vector.tensor_copy(GB8[:, :], GT[:, 8 * tb:8 * tb + 8]))
                w(nc.tensor, s_ve)
                inc(s_pe, nc.tensor.transpose(PSB[0:8, :], GB8[:, :], IDT[:]))
                w(nc.vector, s_pe)
                inc(s_ve, nc.vector.tensor_copy(GTT[:, bsl], PSB[0:8, :]))
            gate_tok = s_ve.n
            w(nc.vector, s_ve)
            inc(s_ve, nc.vector.memset(EOACC[:, :], 0.0))
            for ep in range(4):
                eA, eB = 2 * ep, 2 * ep + 1
                wgb = WG[ep % 2]
                w(nc.sync, s_pe)
                dma(nc.sync, wgb[:], W1_b[128 * ep:128 * ep + 128, :], s_ld)
                w1tok = s_ld.n
                for ehc in range(FC):
                    for hh in range(2):
                        csl = slice(512 * hh, 512 * hh + 512)
                        w(nc.tensor, s_ld, w1tok)
                        w(nc.tensor, s_ve, cont_tok)
                        w(nc.tensor, s_ac)
                        w(nc.tensor, s_ve)
                        nc.tensor.matmul(PSX[:, :], wgb[0:64, 128 * ehc:128 * ehc + 128], CONT2[0:64, csl],
                                         start=True, stop=True, tile_position=(0, 0), skip_group_check=True)
                        mm = nc.tensor.matmul(PSY[:, :], wgb[64:128, 128 * ehc:128 * ehc + 128], CONT2[64:128, csl],
                                              start=True, stop=True, tile_position=(64, 0), skip_group_check=True)
                        inc(s_pe, mm)
                        hA_dst = A[ehc][:, 3072 + 512 * hh:3072 + 512 * hh + 512]
                        hB_dst = A[ehc][:, 2048 + 512 * hh:2048 + 512 * hh + 512]
                        w(nc.vector, s_pe)
                        r1 = nc.vector.tensor_scalar(out=hA_dst, in0=PSX[:, :],
                                                     scalar1=B1C[:, 8 * eA + ehc:8 * eA + ehc + 1],
                                                     scalar2=0.0, op0=ALU.add, op1=ALU.max)
                        w(nc.scalar, s_pe)
                        r2 = nc.scalar.activation(out=hB_dst, in_=PSY[:, :], func=AF.Relu,
                                                  bias=B1C[:, 8 * eB + ehc:8 * eB + ehc + 1])
                        inc(s_ve, r1)
                        inc(s_ac, r2)
                h_ve, h_ac = s_ve.n, s_ac.n
                for sub, (e, hoff) in enumerate(((eA, 3072), (eB, 2048))):
                    wg2b = WG2[sub]
                    w(nc.sync, s_pe)
                    dma(nc.sync, wg2b[:], W2_b[128 * e:128 * e + 128, :], s_ld)
                    w2tok = s_ld.n
                    for hh in range(2):
                        w(nc.tensor, s_ve, gate_tok)
                        w(nc.tensor, s_ve)
                        mm = nc.tensor.matmul(PSO_A[0:64, :], SEL[:, 64 * e:64 * e + 64],
                                              GTT[:, 512 * hh:512 * hh + 512],
                                              start=True, stop=True, skip_group_check=True)
                        inc(s_pe, mm)
                        w(nc.vector, s_pe)
                        inc(s_ve, nc.vector.tensor_copy(GREP[:, :], PSO_A[0:64, :]))
                        grep_tok = s_ve.n
                        w(nc.tensor, s_ld, w2tok)
                        w(nc.tensor, s_ve, h_ve)
                        w(nc.tensor, s_ac, h_ac)
                        w(nc.tensor, s_ve)
                        psy = PSX if hh == 0 else PSY
                        for ehc in range(FC):
                            mm = nc.tensor.matmul(psy[0:64, :], wg2b[:, 64 * ehc:64 * ehc + 64],
                                                  A[ehc][:, hoff + 512 * hh:hoff + 512 * hh + 512],
                                                  start=(ehc == 0), stop=(ehc == FC - 1), skip_group_check=True)
                        inc(s_pe, mm)
                        w(nc.vector, s_pe)
                        w(nc.vector, s_ve, grep_tok)
                        st = nc.vector.scalar_tensor_tensor(out=ET0[:, :],
                                                            in0=psy[0:64, :], scalar=B2C[:, e:e + 1],
                                                            in1=GREP[:, :],
                                                            op0=ALU.add, op1=ALU.mult)
                        inc(s_ve, st)
                        w(nc.vector, s_ve)
                        ac2 = nc.vector.tensor_tensor(out=EOACC[:, 512 * hh:512 * hh + 512],
                                                      in0=EOACC[:, 512 * hh:512 * hh + 512],
                                                      in1=ET0[:, :], op=ALU.add)
                        inc(s_ve, ac2)
                        w(nc.vector, s_ve)
            inc(s_ve, nc.vector.tensor_copy(A[0][0:64, 2048:3072], EOACC[:, :]))
            for hc in range(16):
                wgb = WG2[hc % 2]
                w(nc.sync, s_pe)
                dma(nc.sync, wgb[0:64, 0:128], Wc2s_b[0:64, 128 * hc:128 * hc + 128], s_ld)
                wtok = s_ld.n
                w(nc.tensor, s_ld, wtok)
                w(nc.tensor, s_ve)
                if hc >= 1:
                    w(nc.tensor, s_ac)
                for hh in range(2):
                    mm = nc.tensor.matmul(PS01[:, 512 * hh:512 * hh + 512],
                                          wgb[0:64, 0:128],
                                          A[0][0:64, 2048 + 512 * hh:2048 + 512 * hh + 512],
                                          start=True, stop=True, skip_group_check=True)
                inc(s_pe, mm)
                w(nc.scalar, s_pe)
                sdst = A[hc // 2][:, 1024 * (hc % 2):1024 * (hc % 2) + 1024]
                sg = nc.scalar.activation(out=sdst, in_=PS01[:, :], func=AF.Sigmoid,
                                          bias=BC2S[:, hc:hc + 1])
                inc(s_ac, sg)
            s2_tok = s_ac.n
            for dc in range(FC):
                for wsub in range(2):
                    wfb = WF[wsub]
                    w(nc.sync, s_pe)
                    dma(nc.sync, wfb[:], Wdec_b[128 * dc:128 * dc + 128, 1024 * wsub:1024 * wsub + 1024], s_ld)
                    wtok = s_ld.n
                    w(nc.tensor, s_ld, wtok)
                    w(nc.tensor, s_ac, s2_tok)
                    if dc >= 1 or wsub >= 1:
                        w(nc.tensor, s_ve)
                    for hh in range(2):
                        for hci in range(8):
                            hc = 8 * wsub + hci
                            src_s = A[hc // 2][:, 1024 * (hc % 2) + 512 * hh:1024 * (hc % 2) + 512 * hh + 512]
                            mm = nc.tensor.matmul(PS23[:, 512 * hh:512 * hh + 512],
                                                  wfb[:, 128 * hci:128 * hci + 128], src_s,
                                                  start=(hc == 0), stop=(hc == 15), skip_group_check=True)
                    inc(s_pe, mm)
                w(nc.vector, s_pe)
                dv = nc.vector.tensor_scalar(out=A[dc][:, 3072:4096], in0=PS23[:, :],
                                             scalar1=BDEC[:, dc:dc + 1], scalar2=None, op0=ALU.add)
                inc(s_ve, dv)
            dec_tok = s_ve.n
            for hh in range(2):
                gsl = slice(FTILE * ft + 512 * hh, FTILE * ft + 512 * hh + 512)
                w(nc.scalar, s_ve, dec_tok)
                w(nc.tensor, s_ve, dec_tok)
                stats_tile(lambda c: A[c][:, 3072 + 512 * hh:3072 + 512 * hh + 512],
                           lambda c: A[c][:, 3072 + 512 * hh:3072 + 512 * hh + 512])
                for dc in range(FC):
                    xb = XOLD[dc % 2]
                    w(nc.sync, s_ve)
                    dma(nc.sync, xb[:], xres[128 * dc:128 * dc + 128, gsl], s_ld)
                    xtok = s_ld.n
                    w(nc.vector, s_ve)
                    w(nc.vector, s_st)
                    inc(s_ve, nc.vector.tensor_tensor(out=U1[:, :],
                                                      in0=A[dc][:, 3072 + 512 * hh:3072 + 512 * hh + 512],
                                                      in1=RREP[:, :], op=ALU.mult))
                    w(nc.vector, s_ve)
                    inc(s_ve, nc.vector.tensor_scalar(out=SSB[0][:, :], in0=U1[:, :], scalar1=LNG[:, dc:dc + 1],
                                                      scalar2=LNB[:, dc:dc + 1], op0=ALU.mult, op1=ALU.add))
                    w(nc.vector, s_ve)
                    w(nc.vector, s_ld, xtok)
                    inc(s_ve, nc.vector.tensor_tensor(out=xb[:], in0=SSB[0][:, :], in1=xb[:], op=ALU.add))
                    w(nc.gpsimd, s_ve)
                    dma(nc.gpsimd, dst_dram[128 * dc:128 * dc + 128, gsl], xb[:], s_st)
            fence()

    def attn_b(sub=3):
        for c in range(FC):
            for tb in range(NB):
                w(nc.tensor, s_ve)
                inc(s_pe, nc.tensor.transpose(PSB[:, :], XH[c][:, 128 * tb:128 * tb + 128], IDT[:]))
                w(nc.vector, s_pe)
                inc(s_ve, nc.vector.tensor_copy(A[c][:, 128 * tb:128 * tb + 128], PSB[:, :]))
        fence()
        if sub < 2:
            return
        for pr in range(NPAIR if sub >= 3 else 1):
            w(nc.sync, s_pe)
            dma(nc.sync, WQS[:], qwb_b[128 * pr:128 * pr + 128, :], s_ld)
            qw_tok = s_ld.n
            for it in range(NIT):
                i0 = ITILE * it
                w(nc.tensor, s_ld, qw_tok)
                w(nc.tensor, s_ve)
                for c in range(FC):
                    mm = nc.tensor.matmul(PSX[:, :], WQS[:, 128 * c:128 * c + 128], XH[c][:, i0:i0 + ITILE],
                                          start=(c == 0), stop=(c == FC - 1), skip_group_check=True)
                inc(s_pe, mm)
                w(nc.vector, s_pe)
                inc(s_ve, nc.vector.tensor_copy(QB[:, :], PSX[:, :]))
                qb_tok = s_ve.n
                jbmax = (i0 + ITILE - 1) // 128
                ev_ve = {}
                ev_ac = {}
                av_tok = {}
                w(nc.tensor, s_ve)
                w(nc.tensor, s_ac)
                for jb in range(jbmax + 1):
                    j0 = 128 * jb
                    sc = PS01[:, 0:512] if jb % 2 == 0 else PS01[:, 512:1024]
                    scB = PS23[:, 0:512] if jb % 2 == 0 else PS23[:, 512:1024]
                    w(nc.tensor, s_ve, qb_tok)
                    if jb >= 2:
                        w(nc.tensor, s_ve, ev_ve[jb - 2])
                        w(nc.tensor, s_ac, ev_ac[jb - 2])
                    nc.tensor.matmul(sc, XH[pr][0:64, j0:j0 + 128], QB[0:64, :],
                                     start=True, stop=True, tile_position=(0, 0), skip_group_check=True)
                    mm = nc.tensor.matmul(scB, XH[pr][64:128, j0:j0 + 128], QB[64:128, :],
                                          start=True, stop=True, tile_position=(64, 0), skip_group_check=True)
                    qk_tok = inc(s_pe, mm)
                    pa = PB[jb % 2][:, 0:512]
                    pbb = PB[jb % 2][:, 512:1024]
                    delta = j0 - i0
                    w(nc.vector, s_pe, qk_tok)
                    w(nc.scalar, s_pe, qk_tok)
                    if jb >= 2:
                        w(nc.vector, s_pe, av_tok[jb - 2])
                        w(nc.scalar, s_pe, av_tok[jb - 2])
                    if delta < 0 or SUBMODE == 'nomask':
                        e1 = nc.vector.tensor_copy(pa, sc)
                        e2 = nc.scalar.copy(pbb, scB)
                        ev_ve[jb] = inc(s_ve, e1)
                        ev_ac[jb] = inc(s_ac, e2)
                    else:
                        if delta > 0:
                            nc.vector.memset(pa[:, 0:delta], 0.0)
                            nc.vector.memset(pbb[:, 0:delta], 0.0)
                        nc.vector.scalar_tensor_tensor(out=pa[:, delta:delta + 128],
                                                       in0=sc[:, delta:delta + 128], scalar=1.0,
                                                       in1=TRIL[:, :], op0=ALU.mult, op1=ALU.mult)
                        e1 = nc.vector.scalar_tensor_tensor(out=pbb[:, delta:delta + 128],
                                                            in0=scB[:, delta:delta + 128], scalar=1.0,
                                                            in1=TRIL[:, :], op0=ALU.mult, op1=ALU.mult)
                        if delta + 128 < ITILE:
                            nc.vector.tensor_copy(pa[:, delta + 128:], sc[:, delta + 128:])
                            e1 = nc.vector.tensor_copy(pbb[:, delta + 128:], scB[:, delta + 128:])
                        ev_ve[jb] = inc(s_ve, e1)
                        ev_ac[jb] = s_ac.n
                    w(nc.tensor, s_ve, ev_ve[jb])
                    w(nc.tensor, s_ac, ev_ac[jb])
                    vsA = A[pr][:, 128 * jb:128 * jb + 64]
                    vsB = A[pr][:, 128 * jb + 64:128 * jb + 128]
                    if SUBMODE == 'noav':
                        av_tok[jb] = s_pe.n
                    else:
                        nc.tensor.matmul(PSO_A[0:64, :], vsA, pa, start=(jb == 0), stop=(jb == jbmax),
                                         tile_position=(0, 0), skip_group_check=True)
                        mm = nc.tensor.matmul(PSO_B[64:128, :], vsB, pbb, start=(jb == 0), stop=(jb == jbmax),
                                              tile_position=(0, 64), skip_group_check=True)
                        av_tok[jb] = inc(s_pe, mm)
                xb = XOLD[it % 2]
                w(nc.sync, s_ve)
                dma(nc.sync, xb[:], xres[128 * pr:128 * pr + 128, i0:i0 + ITILE], s_ld)
                xtok = s_ld.n
                w(nc.vector, s_pe, av_tok[jbmax])
                w(nc.vector, s_ld, xtok)
                w(nc.vector, s_st)
                if SUBMODE != 'noav':
                    nc.vector.tensor_tensor(out=xb[0:64, :], in0=PSO_A[0:64, :], in1=xb[0:64, :], op=ALU.add)
                    inc(s_ve, nc.vector.tensor_tensor(out=xb[64:128, :], in0=PSO_B[64:128, :], in1=xb[64:128, :], op=ALU.add))
                else:
                    inc(s_ve, nc.vector.tensor_copy(U1[:, :], xb[:]))
                w(nc.gpsimd, s_ve)
                dma(nc.gpsimd, xres[128 * pr:128 * pr + 128, i0:i0 + ITILE], xb[:], s_st)
                w(nc.tensor, s_ve)
        fence()

    if nphase >= 1:
        ln_pass(xT)
    if nphase >= 2:
        attn_a()
    if nphase >= 3:
        ln_pass(xres)
        ffn(xres)
    if nphase >= 4:
        ln_pass(xres)
        attn_b(sub=(1 if nphase == 4 else (2 if nphase == 41 else 3)))
    if nphase >= 5 and nphase != 41:
        ln_pass(xres)
        ffn(outT)

    if nphase < 6:
        fence()
        dma(nc.sync, outT[:], xres[:], s_st)
    KD = os.environ.get('KDUMP', '')
    if KD:
        fence()
        dmp = {'xh0': XH[0][:, 0:T], 'cont': CONT2[:, :], 'gate': GT[:, :],
               'eoacc': EOACC[:, :], 's0': A[0][:, 0:1024], 'dec0': A[0][:, 3072:4096],
               'rrep': RREP[:, :], 'mrrep': MRREP[:, :]}[KD]
        inc(s_ve, nc.vector.tensor_copy(XOLD[0][0:dmp.shape[0], 0:min(512, dmp.shape[1])], dmp[:, 0:min(512, dmp.shape[1])]))
        w(nc.sync, s_ve)
        dma(nc.sync, outT[0:dmp.shape[0], 0:min(512, dmp.shape[1])], XOLD[0][0:dmp.shape[0], 0:min(512, dmp.shape[1])], s_st)

    w(nc.sync, s_st)
    w(nc.sync, s_ld)
    w(nc.sync, s_gp)

    es.close()
    return nc


def _host_prep(inputs):
    x = np.ascontiguousarray(inputs["L0_attn_a_x"])
    B = x.shape[0]
    hperm = (np.arange(HS) + HS // 2) % HS

    def rearr(wm, oc=FC):
        n = wm.shape[1] // oc
        return np.ascontiguousarray(
            wm.reshape(FC, 128, oc, n).transpose(2, 1, 0, 3).reshape(oc * 128, FC * n))

    wq = inputs["L0_attn_a_wq"].astype(np.float32)
    wk = inputs["L0_attn_a_wk"].astype(np.float32)
    wqp = wq.reshape(D, NH, HS)[:, :, hperm].reshape(D, D)
    wkp = wk.reshape(D, NH, HS)[:, :, hperm].reshape(D, D)
    Wenc = inputs["L0_ffn_W_enc"].astype(np.float32)
    Wdec = inputs["L0_ffn_W_dec"].astype(np.float32)
    W1 = inputs["L0_ffn_W1"].astype(np.float32)
    W2 = inputs["L0_ffn_W2"].astype(np.float32)
    sel = np.zeros((8, 512), np.float32)
    for e in range(8):
        sel[e, 64 * e:64 * e + 64] = 1.0

    com = {
        "wq": rearr(wq), "wqp": rearr(wqp), "wk": rearr(wk), "wkp": rearr(wkp),
        "wv": inputs["L0_attn_a_wv"].astype(np.float32),
        "wo": rearr(inputs["L0_attn_a_wo"].astype(np.float32)),
        "qwb": rearr(inputs["L0_attn_b_queries_weight"].astype(np.float32)),
        "relm": inputs["L0_attn_a_rel_pos_emb"].astype(np.float32),
        "Wenc": np.ascontiguousarray(
            Wenc.reshape(FC, 128, 16, 128).transpose(2, 1, 0, 3).reshape(2048, D)),
        "Ws2c": inputs["L0_ffn_W_s2c"].astype(np.float32),
        "Wr": inputs["L0_ffn_Wr"].astype(np.float32),
        "W1": np.ascontiguousarray(W1.reshape(4, 128, 1024).reshape(512, 1024)),
        "W2": np.ascontiguousarray(W2.reshape(8, FC, 128, HS).transpose(0, 2, 1, 3).reshape(1024, 512)),
        "Wc2s": inputs["L0_ffn_W_c2s"].astype(np.float32),
        "Wdec": np.ascontiguousarray(
            Wdec.reshape(16, 128, FC, 128).transpose(2, 1, 0, 3).reshape(1024, 2048)),
        "bo2": np.ascontiguousarray(inputs["L0_attn_a_bo"].astype(np.float32).reshape(FC, 128).T),
        "benc2": np.ascontiguousarray(inputs["L0_ffn_b_enc"].astype(np.float32).reshape(16, 128).T),
        "bs2c2": np.ascontiguousarray(
            np.concatenate([inputs["L0_ffn_b_s2c"], inputs["L0_ffn_b_s2c"]]).astype(np.float32).reshape(128, 1)),
        "br": inputs["L0_ffn_br"].astype(np.float32).reshape(1, 8),
        "b1c": np.ascontiguousarray(
            inputs["L0_ffn_b1"].astype(np.float32).reshape(8, FC, 128).transpose(2, 0, 1).reshape(128, 64)),
        "b2c": np.ascontiguousarray(inputs["L0_ffn_b2"].astype(np.float32).T),
        "bc2s2": np.ascontiguousarray(inputs["L0_ffn_b_c2s"].astype(np.float32).reshape(16, 128).T),
        "bdec2": np.ascontiguousarray(inputs["L0_ffn_b_dec"].astype(np.float32).reshape(FC, 128).T),
        "lng2": np.ascontiguousarray(inputs["L0_ffn_ln_g"].astype(np.float32).reshape(FC, 128).T),
        "lnb2": np.ascontiguousarray(inputs["L0_ffn_ln_b"].astype(np.float32).reshape(FC, 128).T),
        "idm": np.eye(128, dtype=np.float32),
        "trilm": (np.arange(128)[:, None] <= np.arange(128)[None, :]).astype(np.float32),
        "sel8": sel,
    }
    invf = 1.0 / 10000.0 ** (np.arange(0, HS, 2) / HS)
    ang = np.outer(invf, np.arange(T, dtype=np.float64))
    cosd = np.cos(np.concatenate([ang, ang], 0))
    sind = np.sin(np.concatenate([ang, ang], 0))
    sind[:HS // 2] *= -1.0
    com["cos2"] = np.ascontiguousarray(np.tile(cosd, (2, 1)).astype(np.float32))
    com["sinf2"] = np.ascontiguousarray(np.tile(sind, (2, 1)).astype(np.float32))

    in_maps = []
    for b in range(B):
        m = dict(com)
        m["xT"] = np.ascontiguousarray(x[b].T.astype(np.float32))
        in_maps.append(m)
    return in_maps


def kernel(**inputs):
    in_maps = _host_prep(inputs)
    nc = build_program()
    res = run_bass_kernel_spmd(nc, in_maps, core_ids=list(range(8)))
    out = np.stack([np.ascontiguousarray(np.asarray(r["outT"]).T) for r in res.results], axis=0)
    return out.astype(np.float32)


# version 07
# speedup vs baseline: 1.1824x; 1.1824x over previous
import contextlib
import os

import numpy as np

import concourse.bass as bass
import concourse.mybir as mybir
from concourse.ap import AP
from concourse.bass_utils import run_bass_kernel_spmd

F32 = mybir.dt.float32
BF16 = mybir.dt.bfloat16
U32 = mybir.dt.uint32
AF = mybir.ActivationFunctionType
ALU = mybir.AluOpType
AX = mybir.AxisListType

T = 4096
D = 1024
NH = 16
HS = 64
NPAIR = NH // 2
FC = D // 128
NR = 2047
TMAX = T - 1
C_SHIFT = np.float32(10.0)
SCALE = np.float32(HS ** -0.5)
MW = 2688
EROW0 = 2048
EPS = np.float32(1e-5)
ITILE = 512
NIT = T // ITILE
NB = T // 128
FTILE = 1024
NFT = T // FTILE
GSIZE = 4
NGRP = NH // GSIZE


def _band_j0s(i0):
    return [j0 for j0 in range(0, T, 128) if j0 <= i0 - 1539]


def _mranges():
    need = {}
    for i0 in range(0, T, ITILE):
        for j0 in _band_j0s(i0):
            for s in range(ITILE // 128):
                i0s = i0 + 128 * s
                lo = max(0, j0 - (i0s + 127) + TMAX)
                hi = j0 + 127 - i0s + TMAX
                if i0s in need:
                    need[i0s][0] = min(need[i0s][0], lo)
                    need[i0s][1] = max(need[i0s][1], hi + 1)
                else:
                    need[i0s] = [lo, hi + 1]
    return {k: tuple(v) for k, v in need.items()}


MRANGES = _mranges()


def build_program(nphase=6):
    SUBMODE = os.environ.get('KSUB', '')
    nc = bass.Bass("TRN2", target_bir_lowering=False)

    def din(name, shape, dt=F32):
        return nc.dram_tensor(name, list(shape), dt, kind="ExternalInput").ap()

    def dint(name, shape, dt):
        return nc.dram_tensor(name, list(shape), dt, kind="Internal").ap()

    xT = din("xT", [D, T])
    wq = din("wq", [D, D]); wqp = din("wqp", [D, D])
    wk = din("wk", [D, D]); wkp = din("wkp", [D, D])
    wv = din("wv", [D, D]); wo = din("wo", [D, D])
    relm = din("relm", [NR, HS])
    qwb = din("qwb", [D, D])
    Wenc = din("Wenc", [2048, D])
    Ws2c = din("Ws2c", [2048, HS])
    Wr_in = din("Wr", [HS, 8])
    W1 = din("W1", [512, 1024])
    W2 = din("W2", [1024, 512])
    Wc2s = din("Wc2s", [HS, 2048])
    Wdec = din("Wdec", [1024, 2048])
    bo2 = din("bo2", [128, FC])
    benc2 = din("benc2", [128, 16])
    bs2c2 = din("bs2c2", [128, 1])
    br_in = din("br", [1, 8])
    b1c = din("b1c", [128, 64])
    b2c = din("b2c", [64, 8])
    bc2s2 = din("bc2s2", [128, 16])
    bdec2 = din("bdec2", [128, FC])
    lng2 = din("lng2", [128, FC]); lnb2 = din("lnb2", [128, FC])
    cos2 = din("cos2", [128, T]); sinf2 = din("sinf2", [128, T])
    idm = din("idm", [128, 128])
    trilm = din("trilm", [128, 128])
    sel8 = din("sel8", [8, 512])

    wq_b = dint("wq_b", [D, D], BF16); wqp_b = dint("wqp_b", [D, D], BF16)
    wk_b = dint("wk_b", [D, D], BF16); wkp_b = dint("wkp_b", [D, D], BF16)
    wv_b = dint("wv_b", [D, D], BF16); wo_b = dint("wo_b", [D, D], BF16)
    qwb_b = dint("qwb_b", [D, D], BF16)
    Wenc_b = dint("Wenc_b", [2048, D], BF16)
    Ws2c_b = dint("Ws2c_b", [2048, HS], BF16)
    W1_b = dint("W1_b", [512, 1024], BF16)
    W2_b = dint("W2_b", [1024, 512], BF16)
    Wc2s_b = dint("Wc2s_b", [HS, 2048], BF16)
    Wdec_b = dint("Wdec_b", [1024, 2048], BF16)
    cos_b = dint("cos_b", [128, T], BF16)
    sinf_b = dint("sinf_b", [128, T], BF16)
    rel_tmp = dint("rel_tmp", [64, MW], BF16)
    qrc = dint("qrc", [NH * (T - EROW0), MW], BF16)
    ocat = dint("ocat", [D, T], BF16)
    xres = dint("xres", [D, T], F32)
    outT = nc.dram_tensor("outT", [D, T], F32, kind="ExternalOutput").ap()

    es = contextlib.ExitStack()

    def sbuf(name, shape, dt):
        return es.enter_context(nc.sbuf_tensor(name, list(shape), dt))

    XH = [sbuf(f"xh{c}", [128, T], BF16) for c in range(FC)]
    A = [sbuf(f"ar{c}", [128, 4160], BF16) for c in range(FC)]
    RT2 = sbuf("rt2", [128, MW], BF16)
    QR = sbuf("qr", [128, ITILE], BF16)
    PB = [sbuf(f"pb{i}", [128, 2 * ITILE], BF16) for i in range(2)]
    CTA = [sbuf(f"cta{i}", [128, ITILE], BF16) for i in range(2)]
    CTB = [sbuf(f"ctb{i}", [128, ITILE], BF16) for i in range(2)]
    DG = [sbuf(f"dg{i}", [128, 128], BF16) for i in range(4)]
    SSB = [sbuf(f"ssb{i}", [128, ITILE], F32) for i in range(2)]
    WQS = sbuf("wqs", [128, 1024], BF16)
    WQPS = sbuf("wqps", [128, 1024], BF16)
    WVG = sbuf("wvg", [128, GSIZE * HS * FC], BF16)
    WF = [sbuf(f"wf{i}", [128, 1024], BF16) for i in range(2)]
    WG = [sbuf(f"wg{i}", [128, 1024], BF16) for i in range(2)]
    WG2 = [sbuf(f"wg2{i}", [128, 512], BF16) for i in range(2)]
    CONT2 = sbuf("cont2", [128, FTILE], BF16)
    EOACC = sbuf("eoacc", [64, FTILE], F32)
    GREP = sbuf("grep", [64, 512], F32)
    ET0 = sbuf("et0", [64, 512], F32)
    GTT = sbuf("gtt", [8, FTILE], BF16)
    SEL = sbuf("sel", [8, 512], BF16)
    OA = sbuf("oa", [64, ITILE], BF16)
    OB = sbuf("ob", [64, ITILE], BF16)
    LA = sbuf("la", [1, ITILE], F32)
    LB = sbuf("lb", [1, ITILE], F32)
    XOLD = [sbuf(f"xold{i}", [128, 512], F32) for i in range(2)]
    RREP = sbuf("rrep", [128, 512], BF16)
    MRREP = sbuf("mrrep", [128, 512], BF16)
    SQ = [sbuf(f"sq{i}", [128, 512], BF16) for i in range(2)]
    MU = sbuf("mu", [1, 512], F32); VAR = sbuf("var", [1, 512], F32)
    ONES1 = sbuf("ones1", [1, 128], BF16)
    ONESC = sbuf("onesc", [128, 1], BF16)
    IDT = sbuf("idt", [128, 128], BF16)
    TRIL = sbuf("tril", [128, 128], F32)
    B1C = sbuf("b1c_s", [128, 64], F32)
    B2C = sbuf("b2c_s", [64, 8], F32)
    BC2S = sbuf("bc2s_s", [128, 16], F32)
    BDEC = sbuf("bdec_s", [128, FC], F32)
    LNG = sbuf("lng_s", [128, FC], F32)
    LNB = sbuf("lnb_s", [128, FC], F32)
    BO = sbuf("bo_s", [128, FC], F32)
    BENC = sbuf("benc_s", [128, 16], F32)
    BS2C = sbuf("bs2c_s", [128, 1], F32)
    WRX = sbuf("wrx", [64, 8], BF16)
    BRX = sbuf("brx", [1, 8], BF16)
    GT = sbuf("gt", [128, 64], F32)
    PEXP = sbuf("pexp", [128, 8], F32)
    M8V = sbuf("m8v", [128, 8], F32)
    M8I = sbuf("m8i", [128, 8], U32)
    GM = [sbuf(f"gm{i}", [128, 1], F32) for i in range(3)]
    GB8 = sbuf("gb8", [128, 8], BF16)
    QB = sbuf("qb", [128, ITILE], BF16)
    U1 = sbuf("u1", [128, 512], BF16)
    U2 = sbuf("u2", [128, 512], BF16)
    CST = [sbuf("cst0", [128, 512], BF16)]
    SNT = [sbuf("snt0", [128, 512], BF16)]
    ROW1 = sbuf("row1", [1, 512], BF16)
    ROW2 = sbuf("row2", [1, 512], BF16)

    PS01 = es.enter_context(nc.psum_tensor("ps01", [128, 1024], F32))
    PS23 = es.enter_context(nc.psum_tensor("ps23", [128, 1024], F32))
    PSO_A = es.enter_context(nc.psum_tensor("psoa", [128, 512], F32))
    PSO_B = es.enter_context(nc.psum_tensor("psob", [128, 512], F32))
    PSX = es.enter_context(nc.psum_tensor("psx", [128, 512], F32))
    PSY = es.enter_context(nc.psum_tensor("psy", [128, 512], F32))

    PSB = PSO_B[:, 0:64].bitcast(BF16)

    class Sem:
        __slots__ = ("h", "n")

    def S(name):
        s = Sem(); s.h = es.enter_context(nc.semaphore(name)); s.n = 0
        return s

    s_ld = S("s_ld"); s_gp = S("s_gp"); s_st = S("s_st")
    s_dg = S("s_dg"); s_tp = S("s_tp")
    s_pe = S("s_pe"); s_ac = S("s_ac"); s_ve = S("s_ve")

    es.enter_context(nc.Block())

    def inc(sem, instr, k=1):
        instr.then_inc(sem.h, k)
        sem.n += k
        return sem.n

    def w(eng, sem, val=None):
        v = sem.n if val is None else val
        if v:
            eng.wait_ge(sem.h, v)

    def dma(eng, out, in_, sem, **kw):
        eng.dma_start(out=out, in_=in_, **kw).then_inc(sem.h, 16)
        sem.n += 16
        return sem.n

    def fence():
        for eng in (nc.tensor, nc.vector, nc.scalar, nc.sync, nc.gpsimd):
            for sem in (s_ld, s_gp, s_st, s_dg, s_tp, s_pe, s_ac, s_ve):
                if sem.n:
                    eng.wait_ge(sem.h, sem.n)

    for dst, src in [(wq_b, wq), (wqp_b, wqp), (wk_b, wk), (wkp_b, wkp),
                     (wv_b, wv), (wo_b, wo), (qwb_b, qwb), (Wenc_b, Wenc),
                     (Ws2c_b, Ws2c), (W1_b, W1), (W2_b, W2), (Wc2s_b, Wc2s),
                     (Wdec_b, Wdec), (cos_b, cos2), (sinf_b, sinf2)]:
        dma(nc.gpsimd, dst[:], src[:], s_gp)
    dma(nc.gpsimd, IDT[:], idm[:], s_gp)
    dma(nc.gpsimd, WRX[:], Wr_in[:], s_gp)
    dma(nc.gpsimd, BRX[:], br_in[:], s_gp)
    dma(nc.gpsimd, SEL[:], sel8[:], s_gp)
    for dst, src in [(TRIL, trilm), (BO, bo2), (BENC, benc2), (BS2C, bs2c2),
                     (B1C, b1c), (B2C, b2c), (BC2S, bc2s2), (BDEC, bdec2),
                     (LNG, lng2), (LNB, lnb2)]:
        dma(nc.sync, dst[:], src[:], s_ld)
    w(nc.vector, s_ld)
    nc.vector.memset(ONES1[:], 1.0)
    inc(s_ve, nc.vector.memset(ONESC[:], 1.0))

    w(nc.gpsimd, s_gp)
    w(nc.vector, s_ld)
    inc(s_ve, nc.vector.memset(A[0][:, 0:1024], 0.0))
    w(nc.gpsimd, s_ve)
    for t in range(16):
        r0 = 128 * t
        rcnt = min(128, NR - r0)
        dma(nc.gpsimd, A[0][0:rcnt, 64 * t:64 * t + 64], relm[r0:r0 + rcnt, :], s_gp)
    w(nc.vector, s_gp)
    inc(s_ve, nc.vector.memset(RT2[:], 0.0))
    for t in range(16):
        w(nc.tensor, s_gp)
        w(nc.tensor, s_ve)
        inc(s_pe, nc.tensor.transpose(PSB[0:64, :], A[0][:, 64 * t:64 * t + 64], IDT[:]))
        w(nc.vector, s_pe)
        inc(s_ve, nc.vector.tensor_copy(RT2[0:64, 128 * t:128 * t + 128], PSB[0:64, :]))
        w(nc.tensor, s_ve)
    w(nc.vector, s_ve)
    inc(s_ve, nc.vector.tensor_copy(GM[0][0:64, :], RT2[0:64, 2046:2047]))
    w(nc.vector, s_ve)
    inc(s_ve, nc.vector.tensor_scalar(out=RT2[0:64, 0:NR], in0=RT2[0:64, 0:NR],
                                      scalar1=GM[0][0:64, :], scalar2=float(1.0 / float(SCALE)),
                                      op0=ALU.subtract, op1=ALU.mult))
    w(nc.sync, s_ve)
    dma(nc.sync, rel_tmp[:], RT2[0:64, :], s_st)
    w(nc.sync, s_st)
    dma(nc.sync, RT2[64:128, :], rel_tmp[:], s_ld)
    fence()

    def stats_tile(read_tile, write_t):
        for c in range(FC):
            w(nc.tensor, s_ve)
            mm = nc.tensor.matmul(PSX[0:1, :], ONESC[:], read_tile(c),
                                  start=(c == 0), stop=(c == FC - 1), skip_group_check=True)
        inc(s_pe, mm)
        w(nc.vector, s_pe)
        inc(s_ve, nc.vector.tensor_scalar_mul(MU[0:1, :], PSX[0:1, :], 1.0 / D))
        w(nc.vector, s_ve)
        inc(s_ve, nc.vector.tensor_copy(ROW2[0:1, :], MU[0:1, :]))
        w(nc.tensor, s_ve)
        mm = nc.tensor.matmul(PSY[:, :], ONES1[:], ROW2[0:1, :], start=True, stop=True,
                              skip_group_check=True)
        inc(s_pe, mm)
        w(nc.vector, s_pe)
        inc(s_ve, nc.vector.tensor_copy(MRREP[:, :], PSY[:, :]))
        w(nc.vector, s_ve)
        for c in range(FC):
            t1 = nc.vector.tensor_tensor(out=write_t(c), in0=read_tile(c), in1=MRREP[:, :],
                                         op=ALU.subtract)
        sub_tok = inc(s_ve, t1)
        for c in range(FC):
            w(nc.scalar, s_ve, sub_tok)
            w(nc.scalar, s_pe)
            inc(s_ac, nc.scalar.activation(out=SQ[c % 2][:], in_=write_t(c), func=AF.Square))
            w(nc.tensor, s_ac)
            mm = nc.tensor.matmul(PSY[0:1, :], ONESC[:], SQ[c % 2][:],
                                  start=(c == 0), stop=(c == FC - 1), skip_group_check=True)
            inc(s_pe, mm)
        w(nc.vector, s_pe)
        inc(s_ve, nc.vector.tensor_scalar_mul(VAR[0:1, :], PSY[0:1, :], 1.0 / D))
        w(nc.scalar, s_ve)
        inc(s_ac, nc.scalar.activation(out=VAR[0:1, :], in_=VAR[0:1, :], func=AF.Sqrt, bias=EPS))
        w(nc.vector, s_ac)
        inc(s_ve, nc.vector.reciprocal(VAR[0:1, :], VAR[0:1, :]))
        w(nc.vector, s_ve)
        inc(s_ve, nc.vector.tensor_copy(ROW1[0:1, :], VAR[0:1, :]))
        w(nc.tensor, s_ve)
        mm = nc.tensor.matmul(PSX[:, :], ONES1[:], ROW1[0:1, :], start=True, stop=True,
                              skip_group_check=True)
        inc(s_pe, mm)
        w(nc.vector, s_pe)
        inc(s_ve, nc.vector.tensor_copy(RREP[:, :], PSX[:, :]))
        w(nc.tensor, s_ve)

    def ln_pass(src_dram):
        for c in range(FC):
            dma(nc.gpsimd, A[c][:, 0:T], src_dram[128 * c:128 * c + 128, :], s_gp)
        xt_done = s_gp.n
        for eng in (nc.scalar, nc.tensor, nc.vector):
            w(eng, s_gp, xt_done)
        for tt in range(T // 512):
            sl = slice(512 * tt, 512 * tt + 512)
            stats_tile(lambda c: A[c][:, sl], lambda c: XH[c][:, sl])
            w(nc.vector, s_ve)
            for c in range(FC):
                t2 = nc.vector.tensor_tensor(out=XH[c][:, sl], in0=XH[c][:, sl],
                                             in1=RREP[:, :], op=ALU.mult)
            inc(s_ve, t2)
            w(nc.vector, s_ve)
        fence()

    def proj_rot_tile(wmain, wperm, dst_tile, sl, ttpar):
        cb = CST[0]; sn = SNT[0]
        w(nc.sync, s_ve)
        dma(nc.sync, cb[:], cos_b[:, sl], s_ld)
        dma(nc.sync, sn[:], sinf_b[:, sl], s_ld)
        cs_tok = s_ld.n
        w(nc.tensor, s_ld)
        w(nc.tensor, s_ve)
        for c in range(FC):
            nc.tensor.matmul(PSX[:, :], wmain[:, 128 * c:128 * c + 128], XH[c][:, sl],
                             start=(c == 0), stop=(c == FC - 1), skip_group_check=True)
            mm = nc.tensor.matmul(PSY[:, :], wperm[:, 128 * c:128 * c + 128], XH[c][:, sl],
                                  start=(c == 0), stop=(c == FC - 1), skip_group_check=True)
        inc(s_pe, mm)
        w(nc.vector, s_pe)
        w(nc.vector, s_ld, cs_tok)
        w(nc.vector, s_st)
        nc.vector.tensor_tensor(out=U1[:, :], in0=PSX[:, :], in1=cb[:], op=ALU.mult)
        inc(s_ve, nc.vector.tensor_tensor(out=U2[:, :], in0=PSY[:, :], in1=sn[:], op=ALU.mult))
        w(nc.vector, s_ve)
        inc(s_ve, nc.vector.tensor_tensor(out=dst_tile, in0=U1[:, :], in1=U2[:, :], op=ALU.add))

    def proj_pair_rot(wmain, wperm, dst):
        for tt in range(T // 512):
            proj_rot_tile(wmain, wperm, dst[:, 512 * tt:512 * tt + 512],
                          slice(512 * tt, 512 * tt + 512), tt)

    def attn_a():
        for grp in range(NGRP):
            pairs = list(range((GSIZE // 2) * grp, (GSIZE // 2) * grp + GSIZE // 2))
            for pi, pr in enumerate(pairs):
                w(nc.sync, s_pe)
                dma(nc.sync, WQS[:], wk_b[128 * pr:128 * pr + 128, :], s_ld)
                dma(nc.sync, WQPS[:], wkp_b[128 * pr:128 * pr + 128, :], s_ld)
                proj_pair_rot(WQS, WQPS, A[pi])
            w(nc.vector, s_ve)
            for pi in range(GSIZE // 2):
                v3 = A[GSIZE // 2 + pi][:, 0:NB * 130].rearrange("p (b q) -> p b q", q=130)
                nc.vector.memset(v3[:, :, 64:65], 1.0)
                ms = nc.vector.memset(v3[:, :, 129:130], 1.0)
            inc(s_ve, ms)
            ones_tok = s_ve.n
            gc0 = GSIZE * HS * grp
            w(nc.sync, s_pe)
            for c in range(FC):
                dma(nc.sync, WVG[:, 256 * c:256 * c + 256], wv_b[128 * c:128 * c + 128, gc0:gc0 + 256], s_ld)
            wv_tok = s_ld.n
            for tb in range(NB):
                w(nc.tensor, s_ld, wv_tok)
                w(nc.tensor, s_ve)
                for c in range(FC):
                    mm = nc.tensor.matmul(PSX[:, 0:256], XH[c][:, 128 * tb:128 * tb + 128],
                                          WVG[:, 256 * c:256 * c + 256],
                                          start=(c == 0), stop=(c == FC - 1), skip_group_check=True)
                inc(s_pe, mm)
                w(nc.vector, s_pe)
                w(nc.vector, s_ve, ones_tok)
                for pi in range(GSIZE // 2):
                    vA = A[GSIZE // 2 + pi]
                    nc.vector.tensor_copy(vA[:, 130 * tb:130 * tb + 64], PSX[:, 128 * pi:128 * pi + 64])
                    cp = nc.vector.tensor_copy(vA[:, 130 * tb + 65:130 * tb + 129],
                                               PSX[:, 128 * pi + 64:128 * pi + 128])
                inc(s_ve, cp)
            fence()

            for pi, pr in enumerate(pairs):
                kA = A[pi]
                vA = A[GSIZE // 2 + pi]
                hA, hB = 2 * pr, 2 * pr + 1
                w(nc.sync, s_pe)
                dma(nc.sync, WQS[:], wq_b[128 * pr:128 * pr + 128, :], s_ld)
                dma(nc.sync, WQPS[:], wqp_b[128 * pr:128 * pr + 128, :], s_ld)
                for it in range(NIT):
                    i0 = ITILE * it
                    proj_rot_tile(WQS, WQPS, QR[:, :], slice(i0, i0 + ITILE), it)
                    qr_tok = s_ve.n
                    for s4 in range(ITILE // 128):
                        i0s = i0 + 128 * s4
                        if i0s not in MRANGES:
                            continue
                        mlo, mhi = MRANGES[i0s]
                        mt = (mlo // 512) * 512
                        while mt < mhi:
                            mwd = min(512, MW - mt)
                            w(nc.tensor, s_ve, qr_tok)
                            w(nc.tensor, s_ac)
                            nc.tensor.matmul(PSX[:, 0:mwd], QR[0:64, 128 * s4:128 * s4 + 128],
                                             RT2[0:64, mt:mt + mwd],
                                             start=True, stop=True, tile_position=(0, 0), skip_group_check=True)
                            mm = nc.tensor.matmul(PSY[:, 0:mwd], QR[64:128, 128 * s4:128 * s4 + 128],
                                                  RT2[64:128, mt:mt + mwd],
                                                  start=True, stop=True, tile_position=(64, 0), skip_group_check=True)
                            inc(s_pe, mm)
                            w(nc.scalar, s_pe)
                            w(nc.scalar, s_ve, qr_tok)
                            w(nc.scalar, s_st)
                            nc.scalar.copy(U1[:, 0:mwd], PSX[:, 0:mwd])
                            cp = nc.scalar.copy(U2[:, 0:mwd], PSY[:, 0:mwd])
                            inc(s_ac, cp)
                            w(nc.gpsimd, s_ac)
                            r0 = hA * (T - EROW0) + (i0s - EROW0)
                            dma(nc.gpsimd, qrc[r0:r0 + 128, mt:mt + mwd], U1[:, 0:mwd], s_st)
                            r0 = hB * (T - EROW0) + (i0s - EROW0)
                            dma(nc.gpsimd, qrc[r0:r0 + 128, mt:mt + mwd], U2[:, 0:mwd], s_st)
                            mt += mwd
                    qrc_st_tok = s_st.n
                    band_list = _band_j0s(i0)
                    bandj = set(band_list)
                    ct_tok = {}
                    dve_band_tok = {}

                    def prefetch(bi):
                        j0 = band_list[bi]
                        r = bi % 2
                        if bi >= 2:
                            w(nc.sync, s_ve, dve_band_tok[bi - 2])
                        w(nc.sync, s_st, qrc_st_tok)
                        for head, ctb in ((hA, CTA[r]), (hB, CTB[r])):
                            base = head * (T - EROW0)
                            w(nc.sync, s_tp)
                            for s in range(4):
                                i0s = i0 + 128 * s
                                off = (base + i0s - EROW0) * MW + (j0 - i0s + TMAX)
                                dma(nc.sync, DG[s][:], AP(qrc.tensor, off, [[MW - 1, 128], [1, 128]]), s_dg)
                            w(nc.sync, s_dg)
                            for s in range(4):
                                dma(nc.sync, ctb[:, 128 * s:128 * s + 128], DG[s][:], s_tp, transpose=True)
                        ct_tok[j0] = s_tp.n

                    for bi in range(min(2, len(band_list))):
                        prefetch(bi)

                    w(nc.tensor, s_ac)
                    w(nc.tensor, s_ve, qr_tok)
                    w(nc.scalar, s_pe)
                    exp_tok = {}
                    av_tok = {}

                    def emit_av(j):
                        pbj = PB[j % 2]
                        w(nc.tensor, s_ac, exp_tok[j])
                        nc.tensor.matmul(PSO_A[0:65, :], vA[:, 130 * j:130 * j + 65], pbj[:, 0:512],
                                         start=(j == 0), stop=(j == NB - 1), skip_group_check=True)
                        mm = nc.tensor.matmul(PSO_B[0:65, :], vA[:, 130 * j + 65:130 * j + 130], pbj[:, 512:1024],
                                              start=(j == 0), stop=(j == NB - 1), skip_group_check=True)
                        av_tok[j] = inc(s_pe, mm)

                    for jb in range(NB):
                        j0 = 128 * jb
                        sc = PS01 if jb % 2 == 0 else PS23
                        if jb >= 2:
                            w(nc.tensor, s_ac, exp_tok[jb - 2])
                        nc.tensor.matmul(sc[:, 0:512], kA[0:64, j0:j0 + 128], QR[0:64, :],
                                         start=True, stop=True, tile_position=(0, 0), skip_group_check=True)
                        mm = nc.tensor.matmul(sc[:, 512:1024], kA[64:128, j0:j0 + 128], QR[64:128, :],
                                              start=True, stop=True, tile_position=(64, 0), skip_group_check=True)
                        qk_tok = inc(s_pe, mm)
                        is_band = j0 in bandj
                        if is_band:
                            bi = band_list.index(j0)
                            r = bi % 2
                            w(nc.vector, s_pe, qk_tok)
                            w(nc.vector, s_tp, ct_tok[j0])
                            w(nc.vector, s_ac)
                            nc.vector.tensor_tensor(out=SSB[0][:, :], in0=sc[:, 0:512], in1=CTA[r][:], op=ALU.add)
                            bm = nc.vector.tensor_tensor(out=SSB[1][:, :], in0=sc[:, 512:1024], in1=CTB[r][:], op=ALU.add)
                            band_tok = inc(s_ve, bm)
                            dve_band_tok[bi] = band_tok
                            if bi + 2 < len(band_list):
                                prefetch(bi + 2)
                        pb = PB[jb % 2]
                        w(nc.scalar, s_pe, qk_tok)
                        if jb >= 2:
                            w(nc.scalar, s_pe, av_tok[jb - 2])
                        if is_band:
                            w(nc.scalar, s_ve, band_tok)
                            nc.scalar.activation(out=pb[:, 0:512], in_=SSB[0][:, :], func=AF.Exp,
                                                 bias=-C_SHIFT, scale=SCALE)
                            ex = nc.scalar.activation(out=pb[:, 512:1024], in_=SSB[1][:, :], func=AF.Exp,
                                                      bias=-C_SHIFT, scale=SCALE)
                        else:
                            ex = nc.scalar.activation(out=pb[:, :], in_=sc[:, :], func=AF.Exp,
                                                      bias=-C_SHIFT, scale=SCALE)
                        exp_tok[jb] = inc(s_ac, ex)
                        if jb >= 1:
                            emit_av(jb - 1)
                    emit_av(NB - 1)
                    w(nc.vector, s_pe, av_tok[NB - 1])
                    nc.vector.tensor_copy(LA[0:1, :], PSO_A[64:65, :])
                    inc(s_ve, nc.vector.tensor_copy(LB[0:1, :], PSO_B[64:65, :]))
                    w(nc.vector, s_ve)
                    nc.vector.reciprocal(LA[0:1, :], LA[0:1, :])
                    inc(s_ve, nc.vector.reciprocal(LB[0:1, :], LB[0:1, :]))
                    w(nc.vector, s_ve)
                    nc.vector.tensor_copy(ROW1[0:1, :], LA[0:1, :])
                    inc(s_ve, nc.vector.tensor_copy(ROW2[0:1, :], LB[0:1, :]))
                    w(nc.tensor, s_ve)
                    nc.tensor.matmul(PSX[0:64, :], ONES1[0:1, 0:64], ROW1[0:1, :], start=True, stop=True,
                                     skip_group_check=True)
                    mm = nc.tensor.matmul(PSY[0:64, :], ONES1[0:1, 0:64], ROW2[0:1, :], start=True, stop=True,
                                          skip_group_check=True)
                    inc(s_pe, mm)
                    w(nc.vector, s_pe)
                    nc.vector.tensor_copy(SSB[1][0:64, :], PSX[0:64, :])
                    inc(s_ve, nc.vector.tensor_copy(SSB[0][0:64, :], PSY[0:64, :]))
                    w(nc.vector, s_ve)
                    w(nc.vector, s_st)
                    nc.vector.tensor_tensor(out=OA[:, :], in0=PSO_A[0:64, :], in1=SSB[1][0:64, :], op=ALU.mult)
                    ot = nc.vector.tensor_tensor(out=OB[:, :], in0=PSO_B[0:64, :], in1=SSB[0][0:64, :], op=ALU.mult)
                    inc(s_ve, ot)
                    w(nc.gpsimd, s_ve)
                    dma(nc.gpsimd, ocat[HS * hA:HS * hA + HS, i0:i0 + ITILE], OA[:, :], s_st)
                    dma(nc.gpsimd, ocat[HS * hB:HS * hB + HS, i0:i0 + ITILE], OB[:, :], s_st)
                    w(nc.tensor, s_ve)
            fence()

        for tt in range(T // 512):
            sl = slice(512 * tt, 512 * tt + 512)
            for c in range(FC):
                w(nc.sync, s_pe)
                dma(nc.sync, A[4 + (c % 2)][:, 512 * (c // 2):512 * (c // 2) + 512],
                    ocat[128 * c:128 * c + 128, sl], s_ld)
            oc_tok = s_ld.n
            for od in range(FC):
                w(nc.sync, s_pe)
                dma(nc.sync, WG[od % 2][:], wo_b[128 * od:128 * od + 128, :], s_ld)
                wtok = s_ld.n
                w(nc.tensor, s_ld, wtok)
                w(nc.tensor, s_ld, oc_tok)
                pso = PSX if od % 2 == 0 else PSY
                if od >= 2:
                    w(nc.tensor, s_ve)
                for c in range(FC):
                    mm = nc.tensor.matmul(pso[:, :], WG[od % 2][:, 128 * c:128 * c + 128],
                                          A[4 + (c % 2)][:, 512 * (c // 2):512 * (c // 2) + 512],
                                          start=(c == 0), stop=(c == FC - 1), skip_group_check=True)
                inc(s_pe, mm)
                xb = XOLD[od % 2]
                w(nc.sync, s_ve)
                dma(nc.sync, xb[:], xT[128 * od:128 * od + 128, sl], s_ld)
                xtok = s_ld.n
                w(nc.vector, s_pe)
                w(nc.vector, s_ld, xtok)
                ad = nc.vector.scalar_tensor_tensor(out=xb[:], in0=pso[:, :], scalar=BO[:, od:od + 1],
                                                    in1=xb[:], op0=ALU.add, op1=ALU.add)
                inc(s_ve, ad)
                w(nc.gpsimd, s_ve)
                dma(nc.gpsimd, xres[128 * od:128 * od + 128, sl], xb[:], s_st)
        fence()

    def ffn(dst_dram):
        for ft in range(NFT):
            for hc in range(16):
                wfb = WF[hc % 2]
                w(nc.sync, s_pe)
                dma(nc.sync, wfb[:], Wenc_b[128 * hc:128 * hc + 128, :], s_ld)
                wtok = s_ld.n
                w(nc.tensor, s_ld, wtok)
                if hc >= 2:
                    w(nc.tensor, s_ac)
                for hh in range(2):
                    hsl = slice(FTILE * ft + 512 * hh, FTILE * ft + 512 * hh + 512)
                    for c in range(FC):
                        mm = nc.tensor.matmul(PS01[:, 512 * hh:512 * hh + 512],
                                              wfb[:, 128 * c:128 * c + 128], XH[c][:, hsl],
                                              start=(c == 0), stop=(c == FC - 1), skip_group_check=True)
                inc(s_pe, mm)
                w(nc.scalar, s_pe)
                sdst = A[hc // 2][:, 1024 * (hc % 2):1024 * (hc % 2) + 1024]
                sg = nc.scalar.activation(out=sdst, in_=PS01[:, :], func=AF.Sigmoid,
                                          bias=BENC[:, hc:hc + 1])
                inc(s_ac, sg)
            for hc in range(16):
                wgb = WG2[hc % 2]
                if hc >= 2:
                    w(nc.sync, s_pe)
                dma(nc.sync, wgb[:, 0:64], Ws2c_b[128 * hc:128 * hc + 128, 0:64], s_ld)
                wtok = s_ld.n
                w(nc.tensor, s_ld, wtok)
                w(nc.tensor, s_ac)
                if hc == 0:
                    w(nc.tensor, s_ve)
                ssrc = A[hc // 2][:, 1024 * (hc % 2):1024 * (hc % 2) + 1024]
                for hh in range(2):
                    nc.tensor.matmul(PS23[0:64, 512 * hh:512 * hh + 512], wgb[:, 0:64],
                                     ssrc[:, 512 * hh:512 * hh + 512],
                                     start=(hc == 0), stop=(hc == 15),
                                     tile_position=(0, 0), skip_group_check=True)
                    mm = nc.tensor.matmul(PS01[64:128, 512 * hh:512 * hh + 512], wgb[:, 0:64],
                                          ssrc[:, 512 * hh:512 * hh + 512],
                                          start=(hc == 0), stop=(hc == 15),
                                          tile_position=(0, 64), skip_group_check=True)
                inc(s_pe, mm)
            w(nc.vector, s_pe)
            nc.vector.tensor_scalar(out=CONT2[0:64, :], in0=PS23[0:64, :], scalar1=BS2C[0:64, 0:1],
                                    scalar2=None, op0=ALU.add)
            cv = nc.vector.tensor_scalar(out=CONT2[64:128, :], in0=PS01[64:128, :], scalar1=BS2C[64:128, 0:1],
                                         scalar2=None, op0=ALU.add)
            inc(s_ve, cv)
            cont_tok = s_ve.n
            for tb in range(8):
                bsl = slice(128 * tb, 128 * tb + 128)
                w(nc.tensor, s_ve, cont_tok)
                if tb >= 1:
                    w(nc.tensor, s_ac)
                    w(nc.tensor, s_ve)
                nc.tensor.matmul(PSX[:, 0:8], CONT2[0:64, bsl], WRX[:, :], start=True, stop=False,
                                 skip_group_check=True)
                mm = nc.tensor.matmul(PSX[:, 0:8], ONES1[0:1, 0:128], BRX[0:1, :], start=False, stop=True,
                                      skip_group_check=True)
                inc(s_pe, mm)
                w(nc.vector, s_pe)
                inc(s_ve, nc.vector.reduce_max(GM[0][:, :], PSX[:, 0:8], axis=AX.X))
                w(nc.vector, s_ve)
                inc(s_ve, nc.vector.tensor_scalar_mul(GM[1][:, :], GM[0][:, :], -1.0))
                w(nc.scalar, s_ve)
                ex = nc.scalar.activation(out=PEXP[:, :], in_=PSX[:, 0:8], func=AF.Exp,
                                          bias=GM[1][:, 0:1])
                inc(s_ac, ex)
                w(nc.vector, s_ac)
                nc.vector.max_with_indices(M8V[:, :], M8I[:, :], PEXP[:, :])
                inc(s_ve, nc.vector.tensor_copy(GM[2][:, :], M8V[:, 1:2]))
                w(nc.vector, s_ve)
                inc(s_ve, nc.vector.tensor_tensor(out=GM[0][:, :], in0=M8V[:, 0:1], in1=M8V[:, 1:2], op=ALU.add))
                w(nc.vector, s_ve)
                inc(s_ve, nc.vector.reciprocal(GM[0][:, :], GM[0][:, :]))
                w(nc.vector, s_ve)
                inc(s_ve, nc.vector.tensor_scalar(out=GT[:, 8 * tb:8 * tb + 8], in0=PEXP[:, :],
                                                  scalar1=GM[2][:, 0:1], scalar2=None, op0=ALU.is_ge))
                w(nc.vector, s_ve)
                inc(s_ve, nc.vector.tensor_tensor(out=GT[:, 8 * tb:8 * tb + 8], in0=GT[:, 8 * tb:8 * tb + 8],
                                                  in1=PEXP[:, :], op=ALU.mult))
                w(nc.vector, s_ve)
                inc(s_ve, nc.vector.tensor_scalar(out=GT[:, 8 * tb:8 * tb + 8], in0=GT[:, 8 * tb:8 * tb + 8],
                                                  scalar1=GM[0][:, 0:1], scalar2=None, op0=ALU.mult))
                w(nc.vector, s_ve)
                inc(s_ve, nc.vector.tensor_copy(GB8[:, :], GT[:, 8 * tb:8 * tb + 8]))
                w(nc.tensor, s_ve)
                inc(s_pe, nc.tensor.transpose(PSB[0:8, :], GB8[:, :], IDT[:]))
                w(nc.vector, s_pe)
                inc(s_ve, nc.vector.tensor_copy(GTT[:, bsl], PSB[0:8, :]))
            gate_tok = s_ve.n
            w(nc.vector, s_ve)
            inc(s_ve, nc.vector.memset(EOACC[:, :], 0.0))
            for ep in range(4):
                eA, eB = 2 * ep, 2 * ep + 1
                wgb = WG[ep % 2]
                w(nc.sync, s_pe)
                dma(nc.sync, wgb[:], W1_b[128 * ep:128 * ep + 128, :], s_ld)
                w1tok = s_ld.n
                for ehc in range(FC):
                    for hh in range(2):
                        csl = slice(512 * hh, 512 * hh + 512)
                        w(nc.tensor, s_ld, w1tok)
                        w(nc.tensor, s_ve, cont_tok)
                        w(nc.tensor, s_ac)
                        w(nc.tensor, s_ve)
                        nc.tensor.matmul(PSX[:, :], wgb[0:64, 128 * ehc:128 * ehc + 128], CONT2[0:64, csl],
                                         start=True, stop=True, tile_position=(0, 0), skip_group_check=True)
                        mm = nc.tensor.matmul(PSY[:, :], wgb[64:128, 128 * ehc:128 * ehc + 128], CONT2[64:128, csl],
                                              start=True, stop=True, tile_position=(64, 0), skip_group_check=True)
                        inc(s_pe, mm)
                        hA_dst = A[ehc][:, 3072 + 512 * hh:3072 + 512 * hh + 512]
                        hB_dst = A[ehc][:, 2048 + 512 * hh:2048 + 512 * hh + 512]
                        w(nc.vector, s_pe)
                        r1 = nc.vector.tensor_scalar(out=hA_dst, in0=PSX[:, :],
                                                     scalar1=B1C[:, 8 * eA + ehc:8 * eA + ehc + 1],
                                                     scalar2=0.0, op0=ALU.add, op1=ALU.max)
                        w(nc.scalar, s_pe)
                        r2 = nc.scalar.activation(out=hB_dst, in_=PSY[:, :], func=AF.Relu,
                                                  bias=B1C[:, 8 * eB + ehc:8 * eB + ehc + 1])
                        inc(s_ve, r1)
                        inc(s_ac, r2)
                h_ve, h_ac = s_ve.n, s_ac.n
                for sub, (e, hoff) in enumerate(((eA, 3072), (eB, 2048))):
                    wg2b = WG2[sub]
                    w(nc.sync, s_pe)
                    dma(nc.sync, wg2b[:], W2_b[128 * e:128 * e + 128, :], s_ld)
                    w2tok = s_ld.n
                    for hh in range(2):
                        w(nc.tensor, s_ve, gate_tok)
                        w(nc.tensor, s_ve)
                        mm = nc.tensor.matmul(PSO_A[0:64, :], SEL[:, 64 * e:64 * e + 64],
                                              GTT[:, 512 * hh:512 * hh + 512],
                                              start=True, stop=True, skip_group_check=True)
                        inc(s_pe, mm)
                        w(nc.vector, s_pe)
                        inc(s_ve, nc.vector.tensor_copy(GREP[:, :], PSO_A[0:64, :]))
                        grep_tok = s_ve.n
                        w(nc.tensor, s_ld, w2tok)
                        w(nc.tensor, s_ve, h_ve)
                        w(nc.tensor, s_ac, h_ac)
                        w(nc.tensor, s_ve)
                        psy = PSX if hh == 0 else PSY
                        for ehc in range(FC):
                            mm = nc.tensor.matmul(psy[0:64, :], wg2b[:, 64 * ehc:64 * ehc + 64],
                                                  A[ehc][:, hoff + 512 * hh:hoff + 512 * hh + 512],
                                                  start=(ehc == 0), stop=(ehc == FC - 1), skip_group_check=True)
                        inc(s_pe, mm)
                        w(nc.vector, s_pe)
                        w(nc.vector, s_ve, grep_tok)
                        st = nc.vector.scalar_tensor_tensor(out=ET0[:, :],
                                                            in0=psy[0:64, :], scalar=B2C[:, e:e + 1],
                                                            in1=GREP[:, :],
                                                            op0=ALU.add, op1=ALU.mult)
                        inc(s_ve, st)
                        w(nc.vector, s_ve)
                        ac2 = nc.vector.tensor_tensor(out=EOACC[:, 512 * hh:512 * hh + 512],
                                                      in0=EOACC[:, 512 * hh:512 * hh + 512],
                                                      in1=ET0[:, :], op=ALU.add)
                        inc(s_ve, ac2)
                        w(nc.vector, s_ve)
            inc(s_ve, nc.vector.tensor_copy(A[0][0:64, 2048:3072], EOACC[:, :]))
            for hc in range(16):
                wgb = WG2[hc % 2]
                w(nc.sync, s_pe)
                dma(nc.sync, wgb[0:64, 0:128], Wc2s_b[0:64, 128 * hc:128 * hc + 128], s_ld)
                wtok = s_ld.n
                w(nc.tensor, s_ld, wtok)
                w(nc.tensor, s_ve)
                if hc >= 1:
                    w(nc.tensor, s_ac)
                for hh in range(2):
                    mm = nc.tensor.matmul(PS01[:, 512 * hh:512 * hh + 512],
                                          wgb[0:64, 0:128],
                                          A[0][0:64, 2048 + 512 * hh:2048 + 512 * hh + 512],
                                          start=True, stop=True, skip_group_check=True)
                inc(s_pe, mm)
                w(nc.scalar, s_pe)
                sdst = A[hc // 2][:, 1024 * (hc % 2):1024 * (hc % 2) + 1024]
                sg = nc.scalar.activation(out=sdst, in_=PS01[:, :], func=AF.Sigmoid,
                                          bias=BC2S[:, hc:hc + 1])
                inc(s_ac, sg)
            s2_tok = s_ac.n
            for dc in range(FC):
                for wsub in range(2):
                    wfb = WF[wsub]
                    w(nc.sync, s_pe)
                    dma(nc.sync, wfb[:], Wdec_b[128 * dc:128 * dc + 128, 1024 * wsub:1024 * wsub + 1024], s_ld)
                    wtok = s_ld.n
                    w(nc.tensor, s_ld, wtok)
                    w(nc.tensor, s_ac, s2_tok)
                    if dc >= 1 or wsub >= 1:
                        w(nc.tensor, s_ve)
                    for hh in range(2):
                        for hci in range(8):
                            hc = 8 * wsub + hci
                            src_s = A[hc // 2][:, 1024 * (hc % 2) + 512 * hh:1024 * (hc % 2) + 512 * hh + 512]
                            mm = nc.tensor.matmul(PS23[:, 512 * hh:512 * hh + 512],
                                                  wfb[:, 128 * hci:128 * hci + 128], src_s,
                                                  start=(hc == 0), stop=(hc == 15), skip_group_check=True)
                    inc(s_pe, mm)
                w(nc.vector, s_pe)
                dv = nc.vector.tensor_scalar(out=A[dc][:, 3072:4096], in0=PS23[:, :],
                                             scalar1=BDEC[:, dc:dc + 1], scalar2=None, op0=ALU.add)
                inc(s_ve, dv)
            dec_tok = s_ve.n
            for hh in range(2):
                gsl = slice(FTILE * ft + 512 * hh, FTILE * ft + 512 * hh + 512)
                w(nc.scalar, s_ve, dec_tok)
                w(nc.tensor, s_ve, dec_tok)
                stats_tile(lambda c: A[c][:, 3072 + 512 * hh:3072 + 512 * hh + 512],
                           lambda c: A[c][:, 3072 + 512 * hh:3072 + 512 * hh + 512])
                for dc in range(FC):
                    xb = XOLD[dc % 2]
                    w(nc.sync, s_ve)
                    dma(nc.sync, xb[:], xres[128 * dc:128 * dc + 128, gsl], s_ld)
                    xtok = s_ld.n
                    w(nc.vector, s_ve)
                    w(nc.vector, s_st)
                    inc(s_ve, nc.vector.tensor_tensor(out=U1[:, :],
                                                      in0=A[dc][:, 3072 + 512 * hh:3072 + 512 * hh + 512],
                                                      in1=RREP[:, :], op=ALU.mult))
                    w(nc.vector, s_ve)
                    inc(s_ve, nc.vector.tensor_scalar(out=SSB[0][:, :], in0=U1[:, :], scalar1=LNG[:, dc:dc + 1],
                                                      scalar2=LNB[:, dc:dc + 1], op0=ALU.mult, op1=ALU.add))
                    w(nc.vector, s_ve)
                    w(nc.vector, s_ld, xtok)
                    inc(s_ve, nc.vector.tensor_tensor(out=xb[:], in0=SSB[0][:, :], in1=xb[:], op=ALU.add))
                    w(nc.gpsimd, s_ve)
                    dma(nc.gpsimd, dst_dram[128 * dc:128 * dc + 128, gsl], xb[:], s_st)
            fence()

    def attn_b(sub=3):
        for c in range(FC):
            for tb in range(NB):
                w(nc.tensor, s_ve)
                inc(s_pe, nc.tensor.transpose(PSB[:, :], XH[c][:, 128 * tb:128 * tb + 128], IDT[:]))
                w(nc.vector, s_pe)
                inc(s_ve, nc.vector.tensor_copy(A[c][:, 128 * tb:128 * tb + 128], PSB[:, :]))
        fence()
        if sub < 2:
            return
        for pr in range(NPAIR if sub >= 3 else 1):
            w(nc.sync, s_pe)
            dma(nc.sync, WQS[:], qwb_b[128 * pr:128 * pr + 128, :], s_ld)
            qw_tok = s_ld.n
            for it in range(NIT):
                i0 = ITILE * it
                w(nc.tensor, s_ld, qw_tok)
                w(nc.tensor, s_ve)
                for c in range(FC):
                    mm = nc.tensor.matmul(PSX[:, :], WQS[:, 128 * c:128 * c + 128], XH[c][:, i0:i0 + ITILE],
                                          start=(c == 0), stop=(c == FC - 1), skip_group_check=True)
                inc(s_pe, mm)
                w(nc.vector, s_pe)
                inc(s_ve, nc.vector.tensor_copy(QB[:, :], PSX[:, :]))
                qb_tok = s_ve.n
                jbmax = (i0 + ITILE - 1) // 128
                ev_ve = {}
                ev_ac = {}
                av_tok = {}

                def emit_av_b(j):
                    w(nc.tensor, s_ve, ev_ve[j])
                    w(nc.tensor, s_ac, ev_ac[j])
                    vsA = A[pr][:, 128 * j:128 * j + 64]
                    vsB = A[pr][:, 128 * j + 64:128 * j + 128]
                    if SUBMODE == 'noav':
                        av_tok[j] = s_pe.n
                    else:
                        nc.tensor.matmul(PSO_A[0:64, :], vsA, PB[j % 2][:, 0:512], start=(j == 0),
                                         stop=(j == jbmax), tile_position=(0, 0), skip_group_check=True)
                        mm = nc.tensor.matmul(PSO_B[64:128, :], vsB, PB[j % 2][:, 512:1024], start=(j == 0),
                                              stop=(j == jbmax), tile_position=(0, 64), skip_group_check=True)
                        av_tok[j] = inc(s_pe, mm)

                w(nc.tensor, s_ve)
                w(nc.tensor, s_ac)
                for jb in range(jbmax + 1):
                    j0 = 128 * jb
                    sc = PS01[:, 0:512] if jb % 2 == 0 else PS01[:, 512:1024]
                    scB = PS23[:, 0:512] if jb % 2 == 0 else PS23[:, 512:1024]
                    w(nc.tensor, s_ve, qb_tok)
                    if jb >= 2:
                        w(nc.tensor, s_ve, ev_ve[jb - 2])
                        w(nc.tensor, s_ac, ev_ac[jb - 2])
                    nc.tensor.matmul(sc, XH[pr][0:64, j0:j0 + 128], QB[0:64, :],
                                     start=True, stop=True, tile_position=(0, 0), skip_group_check=True)
                    mm = nc.tensor.matmul(scB, XH[pr][64:128, j0:j0 + 128], QB[64:128, :],
                                          start=True, stop=True, tile_position=(64, 0), skip_group_check=True)
                    qk_tok = inc(s_pe, mm)
                    pa = PB[jb % 2][:, 0:512]
                    pbb = PB[jb % 2][:, 512:1024]
                    delta = j0 - i0
                    w(nc.vector, s_pe, qk_tok)
                    w(nc.scalar, s_pe, qk_tok)
                    if jb >= 2:
                        w(nc.vector, s_pe, av_tok[jb - 2])
                        w(nc.scalar, s_pe, av_tok[jb - 2])
                    if delta < 0 or SUBMODE == 'nomask':
                        e1 = nc.vector.tensor_copy(pa, sc)
                        e2 = nc.scalar.copy(pbb, scB)
                        ev_ve[jb] = inc(s_ve, e1)
                        ev_ac[jb] = inc(s_ac, e2)
                    else:
                        if delta > 0:
                            nc.vector.memset(pa[:, 0:delta], 0.0)
                            nc.vector.memset(pbb[:, 0:delta], 0.0)
                        nc.vector.scalar_tensor_tensor(out=pa[:, delta:delta + 128],
                                                       in0=sc[:, delta:delta + 128], scalar=1.0,
                                                       in1=TRIL[:, :], op0=ALU.mult, op1=ALU.mult)
                        e1 = nc.vector.scalar_tensor_tensor(out=pbb[:, delta:delta + 128],
                                                            in0=scB[:, delta:delta + 128], scalar=1.0,
                                                            in1=TRIL[:, :], op0=ALU.mult, op1=ALU.mult)
                        if delta + 128 < ITILE:
                            nc.vector.tensor_copy(pa[:, delta + 128:], sc[:, delta + 128:])
                            e1 = nc.vector.tensor_copy(pbb[:, delta + 128:], scB[:, delta + 128:])
                        ev_ve[jb] = inc(s_ve, e1)
                        ev_ac[jb] = s_ac.n
                    if jb >= 1:
                        emit_av_b(jb - 1)
                emit_av_b(jbmax)
                xb = XOLD[it % 2]
                w(nc.sync, s_ve)
                dma(nc.sync, xb[:], xres[128 * pr:128 * pr + 128, i0:i0 + ITILE], s_ld)
                xtok = s_ld.n
                w(nc.vector, s_pe, av_tok[jbmax])
                w(nc.vector, s_ld, xtok)
                w(nc.vector, s_st)
                if SUBMODE != 'noav':
                    nc.vector.tensor_tensor(out=xb[0:64, :], in0=PSO_A[0:64, :], in1=xb[0:64, :], op=ALU.add)
                    inc(s_ve, nc.vector.tensor_tensor(out=xb[64:128, :], in0=PSO_B[64:128, :], in1=xb[64:128, :], op=ALU.add))
                else:
                    inc(s_ve, nc.vector.tensor_copy(U1[:, :], xb[:]))
                w(nc.gpsimd, s_ve)
                dma(nc.gpsimd, xres[128 * pr:128 * pr + 128, i0:i0 + ITILE], xb[:], s_st)
                w(nc.tensor, s_ve)
        fence()

    if nphase >= 1:
        ln_pass(xT)
    if nphase >= 2:
        attn_a()
    if nphase >= 3:
        ln_pass(xres)
        ffn(xres)
    if nphase >= 4:
        ln_pass(xres)
        attn_b(sub=(1 if nphase == 4 else (2 if nphase == 41 else 3)))
    if nphase >= 5 and nphase != 41:
        ln_pass(xres)
        ffn(outT)

    if nphase < 6:
        fence()
        dma(nc.sync, outT[:], xres[:], s_st)
    KD = os.environ.get('KDUMP', '')
    if KD:
        fence()
        dmp = {'xh0': XH[0][:, 0:T], 'cont': CONT2[:, :], 'gate': GT[:, :],
               'eoacc': EOACC[:, :], 's0': A[0][:, 0:1024], 'dec0': A[0][:, 3072:4096],
               'rrep': RREP[:, :], 'mrrep': MRREP[:, :]}[KD]
        inc(s_ve, nc.vector.tensor_copy(XOLD[0][0:dmp.shape[0], 0:min(512, dmp.shape[1])], dmp[:, 0:min(512, dmp.shape[1])]))
        w(nc.sync, s_ve)
        dma(nc.sync, outT[0:dmp.shape[0], 0:min(512, dmp.shape[1])], XOLD[0][0:dmp.shape[0], 0:min(512, dmp.shape[1])], s_st)

    w(nc.sync, s_st)
    w(nc.sync, s_ld)
    w(nc.sync, s_gp)

    es.close()
    return nc


def _host_prep(inputs):
    x = np.ascontiguousarray(inputs["L0_attn_a_x"])
    B = x.shape[0]
    hperm = (np.arange(HS) + HS // 2) % HS

    def rearr(wm, oc=FC):
        n = wm.shape[1] // oc
        return np.ascontiguousarray(
            wm.reshape(FC, 128, oc, n).transpose(2, 1, 0, 3).reshape(oc * 128, FC * n))

    wq = inputs["L0_attn_a_wq"].astype(np.float32)
    wk = inputs["L0_attn_a_wk"].astype(np.float32)
    wqp = wq.reshape(D, NH, HS)[:, :, hperm].reshape(D, D)
    wkp = wk.reshape(D, NH, HS)[:, :, hperm].reshape(D, D)
    Wenc = inputs["L0_ffn_W_enc"].astype(np.float32)
    Wdec = inputs["L0_ffn_W_dec"].astype(np.float32)
    W1 = inputs["L0_ffn_W1"].astype(np.float32)
    W2 = inputs["L0_ffn_W2"].astype(np.float32)
    sel = np.zeros((8, 512), np.float32)
    for e in range(8):
        sel[e, 64 * e:64 * e + 64] = 1.0

    com = {
        "wq": rearr(wq), "wqp": rearr(wqp), "wk": rearr(wk), "wkp": rearr(wkp),
        "wv": inputs["L0_attn_a_wv"].astype(np.float32),
        "wo": rearr(inputs["L0_attn_a_wo"].astype(np.float32)),
        "qwb": rearr(inputs["L0_attn_b_queries_weight"].astype(np.float32)),
        "relm": inputs["L0_attn_a_rel_pos_emb"].astype(np.float32),
        "Wenc": np.ascontiguousarray(
            Wenc.reshape(FC, 128, 16, 128).transpose(2, 1, 0, 3).reshape(2048, D)),
        "Ws2c": inputs["L0_ffn_W_s2c"].astype(np.float32),
        "Wr": inputs["L0_ffn_Wr"].astype(np.float32),
        "W1": np.ascontiguousarray(W1.reshape(4, 128, 1024).reshape(512, 1024)),
        "W2": np.ascontiguousarray(W2.reshape(8, FC, 128, HS).transpose(0, 2, 1, 3).reshape(1024, 512)),
        "Wc2s": inputs["L0_ffn_W_c2s"].astype(np.float32),
        "Wdec": np.ascontiguousarray(
            Wdec.reshape(16, 128, FC, 128).transpose(2, 1, 0, 3).reshape(1024, 2048)),
        "bo2": np.ascontiguousarray(inputs["L0_attn_a_bo"].astype(np.float32).reshape(FC, 128).T),
        "benc2": np.ascontiguousarray(inputs["L0_ffn_b_enc"].astype(np.float32).reshape(16, 128).T),
        "bs2c2": np.ascontiguousarray(
            np.concatenate([inputs["L0_ffn_b_s2c"], inputs["L0_ffn_b_s2c"]]).astype(np.float32).reshape(128, 1)),
        "br": inputs["L0_ffn_br"].astype(np.float32).reshape(1, 8),
        "b1c": np.ascontiguousarray(
            inputs["L0_ffn_b1"].astype(np.float32).reshape(8, FC, 128).transpose(2, 0, 1).reshape(128, 64)),
        "b2c": np.ascontiguousarray(inputs["L0_ffn_b2"].astype(np.float32).T),
        "bc2s2": np.ascontiguousarray(inputs["L0_ffn_b_c2s"].astype(np.float32).reshape(16, 128).T),
        "bdec2": np.ascontiguousarray(inputs["L0_ffn_b_dec"].astype(np.float32).reshape(FC, 128).T),
        "lng2": np.ascontiguousarray(inputs["L0_ffn_ln_g"].astype(np.float32).reshape(FC, 128).T),
        "lnb2": np.ascontiguousarray(inputs["L0_ffn_ln_b"].astype(np.float32).reshape(FC, 128).T),
        "idm": np.eye(128, dtype=np.float32),
        "trilm": (np.arange(128)[:, None] <= np.arange(128)[None, :]).astype(np.float32),
        "sel8": sel,
    }
    invf = 1.0 / 10000.0 ** (np.arange(0, HS, 2) / HS)
    ang = np.outer(invf, np.arange(T, dtype=np.float64))
    cosd = np.cos(np.concatenate([ang, ang], 0))
    sind = np.sin(np.concatenate([ang, ang], 0))
    sind[:HS // 2] *= -1.0
    com["cos2"] = np.ascontiguousarray(np.tile(cosd, (2, 1)).astype(np.float32))
    com["sinf2"] = np.ascontiguousarray(np.tile(sind, (2, 1)).astype(np.float32))

    in_maps = []
    for b in range(B):
        m = dict(com)
        m["xT"] = np.ascontiguousarray(x[b].T.astype(np.float32))
        in_maps.append(m)
    return in_maps


def kernel(**inputs):
    in_maps = _host_prep(inputs)
    nc = build_program()
    res = run_bass_kernel_spmd(nc, in_maps, core_ids=list(range(8)))
    out = np.stack([np.ascontiguousarray(np.asarray(r["outT"]).T) for r in res.results], axis=0)
    return out.astype(np.float32)
